# version 33
# baseline (speedup 1.0000x reference)
import numpy as np
import ml_dtypes
from contextlib import ExitStack

import concourse.bass as bass
import concourse.mybir as mybir
from concourse.bass_utils import run_bass_kernel_spmd

F32 = mybir.dt.float32
BF16 = mybir.dt.bfloat16
AF = mybir.ActivationFunctionType
ALU = mybir.AluOpType
AX = mybir.AxisListType

T = 2048
D = 1024
DC = 8
M3 = 3072
NT1 = 512
NTILE1 = T // NT1
NT2 = 256
NTILE2 = T // NT2
NE = 20
NPAIR = NE // 2
EPS = 1e-5


class Sem:
    def __init__(self, h):
        self.h = h
        self.n = 0

    def bump(self, k=1):
        self.n += k
        return self.n


class Prog:
    ENGINES = ("sync", "gpsimd", "tensor", "vector", "scalar")

    def __init__(self):
        self.q = {e: [] for e in self.ENGINES}
        self.lastwait = {e: {} for e in self.ENGINES}

    def op(self, eng, fn, waits=(), incs=(), drain=False, lp=False):
        ws = []
        for (s, v) in waits:
            if v <= 0:
                continue
            if v > self.lastwait[eng].get(id(s), 0):
                ws.append((s, v))
                self.lastwait[eng][id(s)] = v
        self.q[eng].append((ws, fn, list(incs), drain, lp))

    def replay(self, block):
        for eng in self.ENGINES:
            ops = self.q[eng]
            if not ops:
                continue

            def run(e, ops=ops):
                for ws, fn, incs, drain, lp in ops:
                    if drain:
                        e.drain()
                    for s, v in ws:
                        e.wait_ge(s.h, v)
                    if lp:
                        with e.bass.allow_low_precision(reason="intentional bf16"):
                            ins = fn(e)
                    else:
                        ins = fn(e)
                    for s, k in incs:
                        ins.then_inc(s.h, k)

            getattr(block, eng)(run)


DEBUG_TAPS = False


def TS1(c, t):
    return slice(c * T + t * NT1, c * T + (t + 1) * NT1)


def US2(c, u):
    return slice(c * T + u * NT2, c * T + (u + 1) * NT2)


def build_program():
    nc = bass.Bass("TRN2", target_bir_lowering=False)

    xt = nc.dram_tensor("xt", [D, T], F32, kind="ExternalInput")
    wqT = nc.dram_tensor("wqT", [D, M3], F32, kind="ExternalInput")
    gamma_pc = nc.dram_tensor("gamma_pc", [128, DC], F32, kind="ExternalInput")
    gateT = nc.dram_tensor("gateT", [D, 16], F32, kind="ExternalInput")
    eup = nc.dram_tensor("eup", [16, D, 512], F32, kind="ExternalInput")
    edn = nc.dram_tensor("edn", [16, 512, D], F32, kind="ExternalInput")
    sup = nc.dram_tensor("sup", [D, 2048], F32, kind="ExternalInput")
    sdn = nc.dram_tensor("sdn", [2048, D], F32, kind="ExternalInput")
    identc = nc.dram_tensor("identc", [128, 128], F32, kind="ExternalInput")
    onehotc = nc.dram_tensor("onehotc", [16, 16 * 128], F32, kind="ExternalInput")
    out = nc.dram_tensor("out", [D, T], F32, kind="ExternalOutput")
    if DEBUG_TAPS:
        dbg_x0 = nc.dram_tensor("dbg_x0", [128, DC * T], F32, kind="ExternalOutput")
        dbg_x1 = nc.dram_tensor("dbg_x1", [128, DC * T], F32, kind="ExternalOutput")
        dbg_xh2 = nc.dram_tensor("dbg_xh2", [128, DC * T], F32, kind="ExternalOutput")
        dbg_cfm = nc.dram_tensor("dbg_cfm", [16, T], F32, kind="ExternalOutput")
        dbg_scaled0 = nc.dram_tensor("dbg_scaled0", [128, 2 * 4 * NT2], F32, kind="ExternalOutput")
        dbg_cbc0 = nc.dram_tensor("dbg_cbc0", [128, 2 * NT2], F32, kind="ExternalOutput")
        dbg_silu0 = nc.dram_tensor("dbg_silu0", [128, 2 * 4 * NT2], F32, kind="ExternalOutput")
        dbg_x1p = [nc.dram_tensor(f"dbg_x1p{i}", [128, DC * NT2], F32,
                                  kind="ExternalOutput") for i in range(NPAIR)]

    xt_r = xt.rearrange("(c p) t -> p c t", p=128)
    wqT_r = wqT.rearrange("(c p) m -> p c m", p=128)
    gateT_r = gateT.rearrange("(c p) e -> p c e", p=128)
    sup_r = sup.rearrange("(c p) i -> p c i", p=128)
    eup_r = eup.rearrange("e (c p) i -> e p c i", p=128)
    edn_r = edn.rearrange("e (c p) h -> e p c h", p=128)
    out_r = out.rearrange("(c p) t -> p c t", p=128)

    with ExitStack() as persist:
        en = persist.enter_context
        x1 = en(nc.sbuf_tensor("x1", [128, DC * T], F32))
        gamma_sb = en(nc.sbuf_tensor("gamma_sb", [128, DC], F32))
        gate_bf = en(nc.sbuf_tensor("gate_bf", [128, DC * 16], BF16))
        ones_col = en(nc.sbuf_tensor("ones_col", [128, 1], BF16))
        ones_colf = en(nc.sbuf_tensor("ones_colf", [128, 1], F32))
        ones_rb = en(nc.sbuf_tensor("ones_rb", [1, 128], BF16))
        carry = en(nc.sbuf_tensor("carry", [128, DC], F32))
        rows = en(nc.sbuf_tensor("rows", [1, 2 * NT1], F32))
        rows_bf = en(nc.sbuf_tensor("rows_bf", [1, 4 * NT1], BF16))
        combine_fm = en(nc.sbuf_tensor("combine_fm", [16, T], BF16))
        crows = en(nc.sbuf_tensor("crows", [1, 8 * NT1], BF16))
        ph_ps = ExitStack()

        s_in = Sem(en(nc.semaphore("s_in")))
        s_w = Sem(en(nc.semaphore("s_w")))
        s_pe = Sem(en(nc.semaphore("s_pe")))
        s_v = Sem(en(nc.semaphore("s_v")))
        s_a = Sem(en(nc.semaphore("s_a")))
        s_o = Sem(en(nc.semaphore("s_o")))
        s_wp = [Sem(en(nc.semaphore("s_wp0"))), Sem(en(nc.semaphore("s_wp1")))]
        s_xt = [Sem(en(nc.semaphore(f"s_xt{i}"))) for i in range(NTILE1)]
        s_wm = [Sem(en(nc.semaphore(f"s_wm{i}"))) for i in range(3)]
        s_dbg = Sem(en(nc.semaphore("s_dbg"))) if DEBUG_TAPS else None
        s_dbg2 = Sem(en(nc.semaphore("s_dbg2"))) if DEBUG_TAPS else None

        def row(i, n=NT1):
            return rows[0:1, i * NT1:i * NT1 + n]

        def rowb(i, n=NT1):
            return rows_bf[0:1, i * NT1:i * NT1 + n]

        x1v = x1.rearrange("p (c t) -> p c t", c=DC)

        with ExitStack() as phb:
            eb = phb.enter_context
            wq_bf = eb(nc.sbuf_tensor("wq_bf", [128, DC * M3], BF16))
            xhatB = eb(nc.sbuf_tensor("xhatB", [128, 2 * DC * NT1], BF16))
            xst = eb(nc.sbuf_tensor("xst", [128, 2 * DC * NT1], BF16))
            q_bf = eb(nc.sbuf_tensor("q_bf", [128, DC * NT1], BF16))
            kvb = eb(nc.sbuf_tensor("kvb", [128, DC * NT1], BF16))
            yb = eb(nc.sbuf_tensor("yb", [128, DC * NT1], BF16))
            ab = eb(nc.sbuf_tensor("ab", [128, DC * NT1], F32))
            psS = [eb(nc.psum_tensor(f"psS{i}", [128, NT1], F32)) for i in range(2)]
            psQ = [eb(nc.psum_tensor(f"psQ{i}", [128, NT1], F32)) for i in range(4)]
            psBC = [eb(nc.psum_tensor(f"psBC{i}", [128, NT1], F32)) for i in range(2)]

            pb = Prog()
            pa = pb
            pa.op("sync", lambda sy: sy.dma_start(gamma_sb[:, :], gamma_pc[:, :]),
                  incs=[(s_in, 16)])
            tok_gamma = s_in.bump(16)
            for t in range(NTILE1):
                for ci, c in enumerate(range(DC)):
                    eng = "sync" if c % 2 == 0 else "scalar"
                    w = []
                    if t >= 2 and ci < 2:
                        w = [(s_wm[t - 2], 128)]
                    pa.op(eng, lambda e, c=c, t=t: e.dma_start(
                        x1[:, TS1(c, t)], xt_r[:, c, t * NT1:(t + 1) * NT1]),
                        waits=w, incs=[(s_xt[t], 16)])
                    s_xt[t].bump(16)
    
            MG = 4 * 128
            wq_mstarts = [8, 12, 16, 20, 0, 4]
            for g6, ms in enumerate(wq_mstarts):
                sem = s_wm[g6 % 2]
                for c in range(DC):
                    wqw = []
                    if c == 0:
                        if g6 < 2:
                            wqw = [(s_xt[0], s_xt[0].n)]
                        else:
                            wqw = [(sem, 128 * (g6 // 2))]
                    pa.op("gpsimd", lambda g, c=c, ms=ms: g.dma_start(
                        wq_bf[:, c * M3 + ms * 128:c * M3 + ms * 128 + MG],
                        wqT_r[:, c, ms * 128:ms * 128 + MG]),
                        waits=wqw, incs=[(sem, 16)])
                    sem.bump(16)
            pa.op("gpsimd", lambda g: g.dma_start(
                gate_bf.rearrange("p (c e) -> p c e", c=DC), gateT_r[:, :, :]),
                incs=[(s_w, 16)])
            tok_gate = s_w.bump(16)

            pa.op("vector", lambda v: v.memset(ones_col[:, :], 1.0))
            pa.op("vector", lambda v: v.memset(ones_colf[:, :], 1.0))
            pa.op("vector", lambda v: v.memset(ones_rb[:, :], 1.0))
            pa.op("vector", lambda v: v.memset(carry[:, :], 0.0), incs=[(s_v, 1)])
            tok_init = s_v.bump()

            if DEBUG_TAPS:
                pb.op("gpsimd", lambda g: g.dma_start(dbg_x0[:, :], x1[:, :]),
                      waits=[(s_in, tok_x)], incs=[(s_dbg, 16)])
                s_dbg.bump(16)

            cast_tok = [0] * NTILE1
            stats_tok = [0] * NTILE1
            rows_tok = [0] * NTILE1
            rowsA_tok = [0] * NTILE1
            xhat_tok = [0] * NTILE1
            scan_tok = [0] * NTILE1
            bcN_tok = [0] * NTILE1
            bcA_tok = [0] * NTILE1
            qm_tok = {}
            ev_tok = {}
            x1d_tok = [0] * NTILE1

            def XS(c, par):
                return xst[:, (par * DC + c) * NT1:(par * DC + c + 1) * NT1]

            def XH(c, par):
                return xhatB[:, (par * DC + c) * NT1:(par * DC + c + 1) * NT1]

            psfree_tok = [0] * NTILE1
            sx_tok = [0] * NTILE1
            sq_tok = [0] * NTILE1

            def emit_cast(t):
                par = t % 2
                w = [(s_xt[t], s_xt[t].n)]
                if t == 0:
                    w.append((s_v, tok_init))
                if t >= 2:
                    w.append((s_pe, stats_tok[t - 2]))
                for c in range(DC):
                    inc = [(s_a, 1)] if c == DC - 1 else []
                    pb.op("scalar", lambda sc, c=c, t=t, par=par: sc.copy(
                        XS(c, par), x1[:, TS1(c, t)]),
                        waits=w if c == 0 else [], incs=inc)
                cast_tok[t] = s_a.bump()

            def emit_sum(t):
                par = t % 2
                w0 = [(s_a, cast_tok[t])]
                if t >= 2:
                    w0.append((s_v, psfree_tok[t - 2]))
                for c in range(DC):
                    inc = [(s_pe, 1)] if c == DC - 1 else []
                    pb.op("tensor", lambda te, c=c, par=par: te.matmul(
                        psS[par][0:1, :], ones_col[:, :], XS(c, par),
                        start=(c == 0), stop=(c == DC - 1)),
                        waits=w0 if c == 0 else [], incs=inc)
                sx_tok[t] = s_pe.bump()
                wng = [(s_pe, sx_tok[t])]
                if t >= 2:
                    wng.append((s_pe, bcN_tok[t - 2]))
                pb.op("vector", lambda v, par=par: v.tensor_single_scalar(
                    rowb(par), psS[par][0:1, :], -1.0 / D, op=ALU.mult),
                    waits=wng, drain=True, incs=[(s_v, 1)])
                rows_tok[t] = s_v.bump()

            def emit_sq(t):
                par = t % 2
                for c in range(DC):
                    inc = [(s_a, 1)] if c == DC - 1 else []
                    pb.op("scalar", lambda sc, c=c, par=par: sc.activation(
                        XS(c, par), XS(c, par), AF.Square),
                        waits=[(s_pe, sx_tok[t])] if c == 0 else [],
                        incs=inc, drain=(c == 0))
                sq_tok[t] = s_a.bump()

            def emit_sumsq(t):
                par = t % 2
                for c in range(DC):
                    inc = [(s_pe, 1)] if c == DC - 1 else []
                    pb.op("tensor", lambda te, c=c, par=par: te.matmul(
                        psS[par][32:33, :], ones_col[:, :], XS(c, par),
                        start=(c == 0), stop=(c == DC - 1), tile_position=(0, 32)),
                        waits=[(s_a, sq_tok[t])] if c == 0 else [], incs=inc)
                stats_tok[t] = s_pe.bump()

            def emit_var(t):
                par = t % 2
                pb.op("vector", lambda v, par=par: v.tensor_single_scalar(
                    row(0), psS[par][32:33, :], 1.0 / D, op=ALU.mult),
                    waits=[(s_pe, stats_tok[t])], drain=True, incs=[(s_v, 1)])
                psfree_tok[t] = s_v.bump()
                pb.op("vector", lambda v, par=par: v.tensor_mul(
                    row(1), rowb(par), rowb(par)), drain=True)
                pb.op("vector", lambda v: v.tensor_sub(
                    row(0), row(0), row(1)), drain=True, incs=[(s_v, 1)])
                var_tok = s_v.bump()
                pb.op("scalar", lambda sc: sc.activation(row(1), row(0), AF.Sqrt),
                      waits=[(s_v, var_tok)], incs=[(s_a, 1)])
                sv_tok = s_a.bump()
                pb.op("vector", lambda v: v.reciprocal(row(0), row(1)),
                      waits=[(s_a, sv_tok)])
                wcp = []
                if t >= 2:
                    wcp.append((s_pe, bcA_tok[t - 2]))
                pb.op("vector", lambda v, par=par: v.tensor_copy(rowb(2 + par), row(0)),
                      waits=wcp, incs=[(s_v, 1)], drain=True)
                rowsA_tok[t] = s_v.bump()

            def emit_bc(t):
                par = t % 2
                w = [(s_v, rows_tok[t])]
                if t >= 1:
                    w.append((s_v, xhat_tok[t - 1]))
                pb.op("tensor", lambda te, par=par: te.matmul(
                    psBC[0][:, :], ones_rb[0:1, :], rowb(par), start=True, stop=True),
                    waits=w, incs=[(s_pe, 1)])
                bcN_tok[t] = s_pe.bump()
                pb.op("tensor", lambda te, par=par: te.matmul(
                    psBC[1][:, :], ones_rb[0:1, :], rowb(2 + par), start=True, stop=True),
                    waits=[(s_v, rowsA_tok[t])], incs=[(s_pe, 1)])
                bcA_tok[t] = s_pe.bump()

            def emit_xhat(t):
                par = t % 2
                xh_w = [(s_in, tok_gamma), (s_pe, bcN_tok[t])]
                if t >= 2:
                    xh_w.append((s_pe, qm_tok[(t - 2, 7)]))
                for c in range(DC):
                    pb.op("vector", lambda v, c=c, t=t, par=par: v.tensor_tensor(
                        XH(c, par), x1[:, TS1(c, t)], psBC[0][:, :], op=ALU.add),
                        waits=xh_w if c == 0 else [], drain=(c == 0))
                for c in range(DC):
                    inc = [(s_v, 1)] if c == DC - 1 else []
                    wx2 = [(s_pe, bcA_tok[t])] if c == 0 else []
                    pb.op("vector", lambda v, c=c, par=par: v.scalar_tensor_tensor(
                        XH(c, par), XH(c, par), gamma_sb[:, c:c + 1],
                        psBC[1][:, :], op0=ALU.mult, op1=ALU.mult),
                        waits=wx2, incs=inc, drain=(c == 0))
                xhat_tok[t] = s_v.bump()

            def emit_qkva_scan(t, mids=None):
                par = t % 2
                ms_order = list(range(8, 24)) + list(range(0, 8))
                for pi, m in enumerate(ms_order):
                    if mids and pi in mids:
                        mids[pi]()
                    bank = psQ[pi % 4]
                    wq_w = []
                    if pi == 0:
                        wq_w = [(s_v, xhat_tok[t])]
                    if t == 0 and pi % 4 == 0:
                        g6 = pi // 4
                        wq_w.append((s_wm[g6 % 2], 128 * (g6 // 2 + 1)))
                    key = ((t, ms_order[pi - 4]) if pi >= 4 else
                           ((t - 1, ms_order[pi + 20]) if t > 0 else None))
                    if key is not None and key in ev_tok:
                        wq_w.append(ev_tok[key])
                    for k in range(DC):
                        inc = [(s_pe, 1)] if k == DC - 1 else []
                        pb.op("tensor", lambda te, k=k, m=m, bank=bank, par=par: te.matmul(
                            bank[:, :],
                            wq_bf[:, k * M3 + m * 128:k * M3 + (m + 1) * 128],
                            XH(k, par), start=(k == 0), stop=(k == DC - 1)),
                            waits=wq_w if k == 0 else [], incs=inc)
                    qm_tok[(t, m)] = s_pe.bump()
                    if m < 8:
                        wqe = [(s_pe, qm_tok[(t, m)])]
                        if m == 0 and t > 0:
                            wqe.append((s_v, x1d_tok[t - 1]))
                        pb.op("scalar", lambda sc, m=m, bank=bank: sc.copy(
                            q_bf[:, m * NT1:(m + 1) * NT1], bank[:, :]),
                            waits=wqe, incs=[(s_a, 1)])
                        ev_tok[(t, m)] = (s_a, s_a.bump())
                    elif m < 16:
                        c = m - 8
                        wkv = [(s_pe, qm_tok[(t, m)])]
                        if m == 8 and t > 0:
                            wkv.append((s_v, scan_tok[t - 1]))
                        pb.op("scalar", lambda sc, c=c, bank=bank: sc.copy(
                            kvb[:, c * NT1:(c + 1) * NT1], bank[:, :]),
                            waits=wkv, incs=[(s_a, 1)])
                        ev_tok[(t, m)] = (s_a, s_a.bump())
                    else:
                        c = m - 16
                        wa = [(s_pe, qm_tok[(t, m)])]
                        if m == 16 and t > 0:
                            wa.append((s_v, x1d_tok[t - 1]))
                        pb.op("scalar", lambda sc, c=c, bank=bank: sc.activation(
                            ab[:, c * NT1:(c + 1) * NT1], bank[:, :], AF.Sigmoid),
                            waits=wa, incs=[(s_a, 1)],
                            drain=(m == 16))
                    
                        ev_tok[(t, m)] = (s_a, s_a.bump())
                a_done_tok = ev_tok[(t, 23)][1]
                q_done_tok = ev_tok[(t, 7)][1]

                for c in range(DC):
                    inc = [(s_v, 1)] if c == DC - 1 else []
                    pb.op("vector", lambda v, c=c: v.tensor_tensor_scan(
                        yb[:, c * NT1:(c + 1) * NT1], ab[:, c * NT1:(c + 1) * NT1],
                        kvb[:, c * NT1:(c + 1) * NT1], carry[:, c:c + 1],
                        op0=ALU.mult, op1=ALU.add),
                        waits=[(s_a, a_done_tok)] if c == 0 else [],
                        incs=inc, drain=(c == 0))
                scan_tok[t] = s_v.bump()
                for c in range(DC):
                    pb.op("vector", lambda v, c=c: v.tensor_copy(
                        carry[:, c:c + 1], yb[:, (c + 1) * NT1 - 1:(c + 1) * NT1]),
                        drain=(c == 0))
                for c in range(DC):
                    pb.op("vector", lambda v, c=c: v.tensor_mul(
                        ab[:, c * NT1:(c + 1) * NT1],
                        q_bf[:, c * NT1:(c + 1) * NT1],
                        yb[:, c * NT1:(c + 1) * NT1]),
                        waits=[(s_a, q_done_tok)] if c == 0 else [], drain=(c == 0))
                for c in range(DC):
                    inc = [(s_v, 1)] if c == DC - 1 else []
                    wres = []
                    if c == 0 and t == 0 and DEBUG_TAPS:
                        wres = [(s_dbg, 16)]
                    pb.op("vector", lambda v, c=c, t=t: v.tensor_add(
                        x1[:, TS1(c, t)], x1[:, TS1(c, t)], ab[:, c * NT1:(c + 1) * NT1]),
                        waits=wres, incs=inc, drain=(c == 0))
                x1d_tok[t] = s_v.bump()
                return x1d_tok[t]

            c_stats_tok = [0] * NTILE1
            c_rows_tok = [0] * NTILE1
            c_rowsA_tok = [0] * NTILE1

            def cneg(j):
                return crows[0:1, (2 * j) * NT1:(2 * j + 1) * NT1]

            def crst(j):
                return crows[0:1, (2 * j + 1) * NT1:(2 * j + 2) * NT1]

            c_psfree_tok = [0] * NTILE1
            c_sx_tok = [0] * NTILE1
            c_sq_tok = [0] * NTILE1
            c_cast_tok = [0] * NTILE1

            def emit_c_sum_f32(j):
                par = j % 2
                w0 = [(s_v, x1d_tok[j]),
                      (s_v, psfree_tok[2 + par] if j < 2 else c_psfree_tok[j - 2])]
                for c in range(DC):
                    inc = [(s_pe, 1)] if c == DC - 1 else []
                    pb.op("tensor", lambda te, c=c, j=j, par=par: te.matmul(
                        psS[par][0:1, :], ones_colf[:, :], x1[:, TS1(c, j)],
                        start=(c == 0), stop=(c == DC - 1)),
                        waits=w0 if c == 0 else [], incs=inc)
                c_sx_tok[j] = s_pe.bump()
                pb.op("vector", lambda v, j=j, par=par: v.tensor_single_scalar(
                    cneg(j), psS[par][0:1, :], -1.0 / D, op=ALU.mult),
                    waits=[(s_pe, c_sx_tok[j])], drain=True, incs=[(s_v, 1)])
                c_rows_tok[j] = s_v.bump()

            def emit_c_sq_f32(j):
                par = j % 2
                w = [(s_v, x1d_tok[j]),
                     (s_pe, stats_tok[2 + par] if j < 2 else c_stats_tok[j - 2])]
                for c in range(DC):
                    inc = [(s_a, 1)] if c == DC - 1 else []
                    pb.op("scalar", lambda sc, c=c, j=j, par=par: sc.activation(
                        XS(c, par), x1[:, TS1(c, j)], AF.Square),
                        waits=w if c == 0 else [], incs=inc, drain=(c == 0))
                c_sq_tok[j] = s_a.bump()

            def emit_c_cast(j):
                par = j % 2
                w = [(s_v, x1d_tok[j])]
                w.append((s_pe, stats_tok[2 + par] if j < 2 else c_stats_tok[j - 2]))
                for c in range(DC):
                    inc = [(s_a, 1)] if c == DC - 1 else []
                    pb.op("scalar", lambda sc, c=c, j=j, par=par: sc.copy(
                        XS(c, par), x1[:, TS1(c, j)]),
                        waits=w if c == 0 else [], incs=inc)
                c_cast_tok[j] = s_a.bump()

            def emit_c_sum(j):
                par = j % 2
                w0 = [(s_a, c_cast_tok[j])]
                w0.append((s_v, psfree_tok[2 + par] if j < 2 else c_psfree_tok[j - 2]))
                for c in range(DC):
                    inc = [(s_pe, 1)] if c == DC - 1 else []
                    pb.op("tensor", lambda te, c=c, par=par: te.matmul(
                        psS[par][0:1, :], ones_col[:, :], XS(c, par),
                        start=(c == 0), stop=(c == DC - 1)),
                        waits=w0 if c == 0 else [], incs=inc)
                c_sx_tok[j] = s_pe.bump()
                pb.op("vector", lambda v, j=j, par=par: v.tensor_single_scalar(
                    cneg(j), psS[par][0:1, :], -1.0 / D, op=ALU.mult),
                    waits=[(s_pe, c_sx_tok[j])], drain=True, incs=[(s_v, 1)])
                c_rows_tok[j] = s_v.bump()

            def emit_c_sq(j):
                par = j % 2
                for c in range(DC):
                    inc = [(s_a, 1)] if c == DC - 1 else []
                    pb.op("scalar", lambda sc, c=c, par=par: sc.activation(
                        XS(c, par), XS(c, par), AF.Square),
                        waits=[(s_pe, c_sx_tok[j])] if c == 0 else [],
                        incs=inc, drain=(c == 0))
                c_sq_tok[j] = s_a.bump()

            def emit_c_sumsq(j):
                par = j % 2
                for c in range(DC):
                    inc = [(s_pe, 1)] if c == DC - 1 else []
                    pb.op("tensor", lambda te, c=c, par=par: te.matmul(
                        psS[par][32:33, :], ones_col[:, :], XS(c, par),
                        start=(c == 0), stop=(c == DC - 1), tile_position=(0, 32)),
                        waits=[(s_a, c_sq_tok[j])] if c == 0 else [], incs=inc)
                c_stats_tok[j] = s_pe.bump()

            def emit_c_var(j):
                par = j % 2
                pb.op("vector", lambda v, par=par: v.tensor_single_scalar(
                    row(0), psS[par][32:33, :], 1.0 / D, op=ALU.mult),
                    waits=[(s_pe, c_stats_tok[j])], drain=True, incs=[(s_v, 1)])
                c_psfree_tok[j] = s_v.bump()
                pb.op("vector", lambda v, j=j: v.tensor_mul(
                    row(1), cneg(j), cneg(j)), drain=True)
                pb.op("vector", lambda v: v.tensor_sub(
                    row(0), row(0), row(1)), drain=True)
                pb.op("vector", lambda v: v.tensor_single_scalar(
                    row(0), row(0), EPS, op=ALU.add), incs=[(s_v, 1)], drain=True)
                vpe_tok = s_v.bump()
                pb.op("scalar", lambda sc: sc.activation(row(1), row(0), AF.Sqrt),
                      waits=[(s_v, vpe_tok)], incs=[(s_a, 1)])
                sd_tok = s_a.bump()
                pb.op("vector", lambda v: v.reciprocal(row(0), row(1)),
                      waits=[(s_a, sd_tok)])
                pb.op("vector", lambda v, j=j: v.tensor_copy(crst(j), row(0)),
                      incs=[(s_v, 1)], drain=True)
                c_rowsA_tok[j] = s_v.bump()


            emit_cast(0)
            emit_sum(0)
            emit_sq(0)
            emit_sumsq(0)
            emit_var(0)
            emit_cast(1)
            emit_sum(1)
            emit_sq(1)
            emit_sumsq(1)
            emit_var(1)
            emit_bc(0)
            emit_xhat(0)
            emit_bc(1)
            emit_xhat(1)
            emit_cast(2)
            emit_qkva_scan(0, mids={
                8: lambda: (emit_cast(3), emit_sum(2), emit_sq(2)),
                16: lambda: (emit_sumsq(2), emit_var(2), emit_sum(3),
                             emit_sq(3))})
            emit_sumsq(3)
            emit_var(3)
            emit_bc(2)
            emit_xhat(2)
            emit_qkva_scan(1, mids={20: lambda: emit_c_cast(0)})
            emit_c_sum(0)
            emit_c_sq(0)
            emit_bc(3)
            emit_xhat(3)
            emit_qkva_scan(2, mids={
                12: lambda: (emit_c_sumsq(0), emit_c_var(0)),
                20: lambda: emit_c_cast(1)})
            emit_c_sum(1)
            emit_c_sq(1)
            x1_done_tok = emit_qkva_scan(3, mids={
                12: lambda: (emit_c_sumsq(1), emit_c_var(1))})
            emit_c_sum_f32(2)
            emit_c_sq_f32(2)
            emit_c_sumsq(2)
            emit_c_var(2)
            emit_c_sum_f32(3)
            emit_c_sq_f32(3)
            emit_c_sumsq(3)
            emit_c_var(3)

            with nc.Block("block1") as blk:
                pb.replay(blk)

        with ExitStack() as phcd:
            ec = phcd.enter_context
            xhat2 = ec(nc.sbuf_tensor("xhat2", [128, DC * T], BF16))
            wup = [ec(nc.sbuf_tensor(f"wup{i}", [128, 2 * DC * 512], BF16)) for i in range(2)]
            wdn = [ec(nc.sbuf_tensor(f"wdn{i}", [128, 2 * 4 * D], BF16)) for i in range(2)]
            wpair_tok = [0] * NPAIR

            def emit_pair_dma(pg, p, waits=()):
                parity = p % 2
                sw = s_wp[parity]
                first = [list(waits)]
                for e01 in range(2):
                    e = 2 * p + e01
                    updst = wup[parity].rearrange("p (e c i) -> p e c i", e=2, c=DC)
                    dndst = wdn[parity].rearrange("p (e c h) -> p e c h", e=2, c=4)
                    if e < 16:
                        pg.op("gpsimd", lambda g, e=e, e01=e01, updst=updst: g.dma_start(
                            updst[:, e01, :, :], eup_r[e]),
                            waits=first[0], incs=[(sw, 16)])
                        first[0] = []
                        sw.bump(16)
                        pg.op("gpsimd", lambda g, e=e, e01=e01, dndst=dndst: g.dma_start(
                            dndst[:, e01, :, :], edn_r[e]), incs=[(sw, 16)])
                        sw.bump(16)
                    else:
                        P_ = e - 16
                        pg.op("gpsimd", lambda g, P_=P_, e01=e01, updst=updst: g.dma_start(
                            updst[:, e01, :, :], sup_r[:, :, P_ * 512:(P_ + 1) * 512]),
                            waits=first[0], incs=[(sw, 16)])
                        first[0] = []
                        sw.bump(16)
                        for c4 in range(4):
                            pg.op("gpsimd", lambda g, P_=P_, e01=e01, c4=c4, parity=parity:
                                  g.dma_start(
                                      wdn[parity][:, (e01 * 4 + c4) * D:(e01 * 4 + c4 + 1) * D],
                                      sdn[P_ * 512 + c4 * 128:P_ * 512 + (c4 + 1) * 128, :]),
                                  incs=[(sw, 16)])
                            sw.bump(16)
                wpair_tok[p] = sw.n

            pc = Prog()
            if DEBUG_TAPS:
                pc.op("vector", lambda v: v.nop(), waits=[(s_dbg, s_dbg.n)])
                pc.op("scalar", lambda sc: sc.nop(), waits=[(s_dbg, s_dbg.n)])
                pc.op("tensor", lambda te: te.nop(), waits=[(s_dbg, s_dbg.n)])
                pc.op("gpsimd", lambda g: g.dma_start(dbg_x1[:, :], x1[:, :]),
                      incs=[(s_dbg2, 16)])
                s_dbg2.bump(16)
            pc.op("gpsimd", lambda g: g.dma_start(ident_sb[:, :], identc[:, :]),
                  incs=[(s_wm[1], 16)])
            tok_ident = s_wm[1].bump(16)
            pc.op("gpsimd", lambda g: g.dma_start(onehot_sb[:, :], onehotc[:, :]),
                  incs=[(s_w, 16)])
            tok_onehot = s_w.bump(16)
            tok_gate = tok_onehot
            emit_pair_dma(pc, 0, waits=[(s_dbg, s_dbg.n)] if DEBUG_TAPS else ())
            emit_pair_dma(pc, 1)

            psU = [ec(nc.psum_tensor(f"psU{i}", [128, 512], F32)) for i in range(4)]
            silu_sb = ec(nc.sbuf_tensor("silu_sb", [128, 2 * 2 * 4 * NT2], BF16))
            onehot_sb = ec(nc.sbuf_tensor("onehot_sb", [16, 16 * 128], BF16))
            early_silu = {}
            psG = ph_ps.enter_context(nc.psum_tensor("psG", [128, 512], F32))
            psTR = ph_ps.enter_context(nc.psum_tensor("psTR", [16, 128], BF16))
            psCB = [ph_ps.enter_context(nc.psum_tensor(f"psCB{i}", [128, NT1], F32))
                    for i in range(2)]
            with ExitStack() as phc:
                ecc = phc.enter_context
                rt = ecc(nc.sbuf_tensor("rt", [128, 768], F32))
                scores_all = ecc(nc.sbuf_tensor("scores_all", [128, 256], F32))
                ident_sb = ecc(nc.sbuf_tensor("ident_sb", [128, 128], BF16))
                mx8_all = ecc(nc.sbuf_tensor("mx8_all", [128, 128], F32))
                comb_all = ecc(nc.sbuf_tensor("comb_all", [128, 256], BF16))

                sc_toks = [0]

                xh2_prev = [0]
                xh2_toks = [0] * NTILE1

                xh2_addprev = [0]

                def emit_cprime(j):
                    wN = [(s_v, xh2_addprev[0])] if xh2_addprev[0] else []
                    pc.op("tensor", lambda te, j=j: te.matmul(
                        psCB[0][:, :], ones_rb[0:1, :], cneg(j),
                        start=True, stop=True), waits=wN, incs=[(s_pe, 1)])
                    cbN = s_pe.bump()
                    wA = [(s_v, xh2_prev[0])] if xh2_prev[0] else []
                    pc.op("tensor", lambda te, j=j: te.matmul(
                        psCB[1][:, :], ones_rb[0:1, :], crst(j),
                        start=True, stop=True), waits=wA, incs=[(s_pe, 1)])
                    cbA = s_pe.bump()
                    HT = NT1 // 2
                    half_tok = [0, 0]
                    for hf in range(2):
                        for c in range(DC):
                            pc.op("vector", lambda v, c=c, j=j, hf=hf: v.tensor_tensor(
                                xhat2[:, c * T + j * NT1 + hf * HT:
                                      c * T + j * NT1 + (hf + 1) * HT],
                                x1[:, c * T + j * NT1 + hf * HT:
                                   c * T + j * NT1 + (hf + 1) * HT],
                                psCB[0][:, hf * HT:(hf + 1) * HT], op=ALU.add),
                                waits=[(s_pe, cbN)] if (c == 0 and hf == 0) else [],
                                drain=(c == 0))
                        for c in range(DC):
                            inc = [(s_v, 1)] if c == DC - 1 else []
                            pc.op("vector", lambda v, c=c, j=j, hf=hf: v.tensor_tensor(
                                xhat2[:, c * T + j * NT1 + hf * HT:
                                      c * T + j * NT1 + (hf + 1) * HT],
                                xhat2[:, c * T + j * NT1 + hf * HT:
                                      c * T + j * NT1 + (hf + 1) * HT],
                                psCB[1][:, hf * HT:(hf + 1) * HT], op=ALU.mult),
                                waits=[(s_pe, cbA)] if (c == 0 and hf == 0) else [],
                                incs=inc, drain=(c == 0))
                        half_tok[hf] = s_v.bump()
                        for s4i in range(2 * hf, 2 * hf + 2):
                            sub = 4 * j + s4i
                            wsc = [(s_v, half_tok[hf]), (s_w, tok_gate)]
                            if sc_toks[0]:
                                wsc.append((s_a, sc_toks[0]))
                            for k in range(DC):
                                inc = [(s_pe, 1)] if k == DC - 1 else []
                                pc.op("tensor", lambda te, k=k, sub=sub: te.matmul(
                                    psG[:, sub * 16:(sub + 1) * 16],
                                    xhat2[:, k * T + sub * 128:k * T + (sub + 1) * 128],
                                    gate_bf[:, k * 16:(k + 1) * 16],
                                    start=(k == 0), stop=(k == DC - 1)),
                                    waits=wsc if k == 0 else [], incs=inc)
                            gm_tok = s_pe.bump()
                            pc.op("scalar", lambda sc, sub=sub: sc.activation(
                                scores_all[:, sub * 16:(sub + 1) * 16],
                                psG[:, sub * 16:(sub + 1) * 16], AF.Sigmoid),
                                waits=[(s_pe, gm_tok)], incs=[(s_a, 1)])
                            sc_toks[0] = s_a.bump()
                    xh2_addprev[0] = half_tok[1]
                    xh2j = half_tok[1]
                    xh2_prev[0] = xh2j
                    xh2_toks[j] = xh2j

                early_upb = {}

                def emit_early_up(u):
                    pu = u % 2
                    for e01 in range(2):
                        for ip in range(2):
                            b = 2 * e01 + ip
                            wup_w = [(s_wp[0], wpair_tok[0]), (s_v, xh2_toks[0])]
                            if (u - 1, b) in early_silu:
                                wup_w.append(early_silu[(u - 1, b)])
                            for i2 in range(2):
                                i = 2 * ip + i2
                                for k in range(DC):
                                    first = (i2 == 0 and k == 0)
                                    last = (i2 == 1 and k == DC - 1)
                                    inc = [(s_pe, 1)] if last else []
                                    pc.op("tensor",
                                          lambda te, e01=e01, i=i, k=k, i2=i2, b=b,
                                          u=u, first=first, last=last:
                                          te.matmul(
                                              psU[b][:, i2 * NT2:(i2 + 1) * NT2],
                                              wup[0][:, (e01 * DC + k) * 512 + i * 128:
                                                     (e01 * DC + k) * 512 + (i + 1) * 128],
                                              xhat2[:, k * T + u * NT2:k * T + (u + 1) * NT2],
                                              start=first, stop=last),
                                          waits=wup_w if first else [], incs=inc)
                            early_upb[(u, b)] = s_pe.bump()

                def emit_early_silu(u):
                    pu = u % 2
                    for b in range(4):
                        e01, ip = b // 2, b % 2
                        pc.op("scalar", lambda sc, e01=e01, ip=ip, b=b, pu=pu, u=u:
                              sc.activation(
                                  silu_sb[:, (pu * 8 + e01 * 4 + 2 * ip) * NT2:
                                          (pu * 8 + e01 * 4 + 2 * ip + 2) * NT2],
                                  psU[b][:, :], AF.Silu),
                              waits=[(s_pe, early_upb[(u, b)])], incs=[(s_a, 1)])
                        early_silu[(u, b)] = (s_a, s_a.bump())

                emit_cprime(0)
                emit_cprime(1)
                emit_cprime(2)
                emit_cprime(3)
                emit_early_up(0)
                sc_tok = sc_toks[0]
                xhat2_tok = s_v.n

                def rv(i0, n):
                    return rt[:, i0:i0 + n]

                def rv3(i0, n, k):
                    return rt[:, i0:i0 + n].rearrange("p (a k) -> p a k", k=k)

                s3 = scores_all.rearrange("p (jg k) -> p jg k", k=4)
                pr = pc
                pr.op("vector", lambda v: v.tensor_add(
                    rv3(0, 128, 2), s3[:, :, 0:2], s3[:, :, 2:4]),
                    waits=[(s_a, sc_tok)], drain=True)
                pr.op("vector", lambda v: v.tensor_add(
                    rv(128, 64), s3[:, :, 0:1], s3[:, :, 1:2]))
                pr.op("vector", lambda v: v.tensor_add(
                    rv(192, 64), s3[:, :, 2:3], s3[:, :, 3:4]))
                pr.op("vector", lambda v: v.tensor_add(
                    rv(256, 64), s3[:, :, 0:1], s3[:, :, 3:4]))
                pr.op("vector", lambda v: v.tensor_add(
                    rv(320, 64), s3[:, :, 1:2], s3[:, :, 2:3]))
                pr.op("vector", lambda v: v.tensor_max(
                    rv(384, 64), rv3(0, 128, 2)[:, :, 0:1], rv3(0, 128, 2)[:, :, 1:2]),
                    drain=True)
                pr.op("vector", lambda v: v.tensor_max(rv(448, 64), rv(128, 64), rv(192, 64)))
                pr.op("vector", lambda v: v.tensor_max(rv(512, 64), rv(256, 64), rv(320, 64)))
                pr.op("vector", lambda v: v.tensor_max(rv(576, 64), rv(384, 64), rv(448, 64)),
                      drain=True)
                pr.op("vector", lambda v: v.tensor_max(rv(640, 64), rv(576, 64), rv(512, 64)),
                      drain=True)
                G4 = rv3(640, 64, 4)
                for gix in range(4):
                    pr.op("vector", lambda v, gix=gix: v.tensor_tensor(
                        rv(gix * 64, 64), G4,
                        G4[:, :, gix:gix + 1].to_broadcast([128, 16, 4]),
                        op=ALU.is_lt), drain=(gix == 0))
                pr.op("vector", lambda v: v.tensor_add(rv(256, 64), rv(0, 64), rv(64, 64)),
                      drain=True)
                pr.op("vector", lambda v: v.tensor_add(rv(320, 64), rv(128, 64), rv(192, 64)))
                pr.op("vector", lambda v: v.tensor_add(rv(256, 64), rv(256, 64), rv(320, 64)),
                      drain=True)
                pr.op("vector", lambda v: v.tensor_single_scalar(
                    rv(320, 64), rv(256, 64), 2.0, op=ALU.is_lt), drain=True)
                pr.op("vector", lambda v: v.tensor_tensor(
                    rv3(0, 256, 4), s3,
                    rv3(320, 64, 1).to_broadcast([128, 64, 4]), op=ALU.mult),
                    drain=True)
                for jj in range(16):
                    pr.op("vector", lambda v, jj=jj: v.max(
                        out=mx8_all[:, jj * 8:(jj + 1) * 8], in_=rv(jj * 16, 16)),
                        drain=(jj == 0))
                th = mx8_all.rearrange("p (j w) -> p j w", w=8)
                ms3 = rv3(0, 256, 16)
                pr.op("vector", lambda v: v.tensor_tensor(
                    rv3(256, 256, 16), ms3,
                    th[:, :, 3:4].to_broadcast([128, 16, 16]), op=ALU.is_ge),
                    drain=True)
                pr.op("vector", lambda v: v.tensor_mul(
                    rv(256, 256), rv(256, 256), rv(0, 256)), drain=True)
                W3 = rv3(256, 256, 16)
                pr.op("vector", lambda v: v.tensor_add(
                    rv3(512, 128, 8), W3[:, :, 0:8], W3[:, :, 8:16]), drain=True)
                d8 = rv3(512, 128, 8)
                pr.op("vector", lambda v: v.tensor_add(
                    rv3(640, 64, 4), d8[:, :, 0:4], d8[:, :, 4:8]), drain=True)
                d4 = rv3(640, 64, 4)
                pr.op("vector", lambda v: v.tensor_add(
                    rv3(704, 32, 2), d4[:, :, 0:2], d4[:, :, 2:4]), drain=True)
                d2 = rv3(704, 32, 2)
                pr.op("vector", lambda v: v.tensor_add(
                    rv(736, 16), d2[:, :, 0:1], d2[:, :, 1:2]), drain=True)
                pr.op("vector", lambda v: v.tensor_single_scalar(
                    rv(736, 16), rv(736, 16), 1e-20, op=ALU.add), drain=True)
                pr.op("vector", lambda v: v.reciprocal(rv(752, 16), rv(736, 16)),
                      drain=True)
                pr.op("vector", lambda v: v.tensor_single_scalar(
                    rv(752, 16), rv(752, 16), 2.5, op=ALU.mult), drain=True)
                pr.op("vector", lambda v: v.tensor_tensor(
                    comb_all.rearrange("p (j e) -> p j e", e=16), W3,
                    rv3(752, 16, 1).to_broadcast([128, 16, 16]), op=ALU.mult),
                    incs=[(s_v, 1)], drain=True)
                comb_tok = s_v.bump()

                emit_early_silu(0)
                emit_early_up(1)
                emit_early_silu(1)

                tr_free = 0
                for jj in range(16):
                    wtr = [(s_v, comb_tok), (s_wm[1], tok_ident)]
                    if tr_free:
                        wtr.append((s_v, tr_free))
                    pc.op("tensor", lambda te, jj=jj: te.transpose(
                        psTR[:, :], comb_all[:, jj * 16:(jj + 1) * 16], ident_sb[:, :]),
                        waits=wtr, incs=[(s_pe, 1)])
                    trm = s_pe.bump()
                    pc.op("vector", lambda v, jj=jj: v.tensor_copy(
                        combine_fm[0:16, jj * 128:(jj + 1) * 128], psTR[:, :]),
                        waits=[(s_pe, trm)], incs=[(s_v, 1)], drain=(jj == 0))
                    tr_free = s_v.bump()
                combine_done_tok = s_v.n

                with nc.Block("ffn_prepass") as blk:
                    pc.replay(blk)
            ph_ps.close()

            pd = Prog()
            if DEBUG_TAPS:
                pd.op("gpsimd", lambda g: g.dma_start(dbg_xh2[:, :], xhat2[:, :]),
                      waits=[(s_v, xhat2_tok), (s_dbg2, s_dbg2.n)], incs=[(s_dbg2, 16)])
                s_dbg2.bump(16)
                pd.op("gpsimd", lambda g: g.dma_start(dbg_cfm[:, :], combine_fm[:, :]),
                      waits=[(s_dbg2, s_dbg2.n)], incs=[(s_dbg2, 16)])
                s_dbg2.bump(16)
            with ExitStack() as phd:
                ed = phd.enter_context
                scaled_sb = ed(nc.sbuf_tensor("scaled_sb", [128, 2 * 8 * NT2], BF16))
                cbc_bf = ed(nc.sbuf_tensor("cbc_bf", [128, 4 * NT2], BF16))
                psM = [ed(nc.psum_tensor(f"psM{i}", [128, 512], F32)) for i in range(4)]

                moe_evac_tok = 0
                silu_tok = {i: (s_a, 0) for i in range(4)}
                cbc_evac_tok = [0, 0]
                cbc_evac_last = [0]
                up_bank_tok = {}
                scaled_tok = [0, 0]
                down_pe_tok = [0, 0]
                last_mult_tok = 0

                def SCL(i, pu):
                    return scaled_sb[:, (pu * 8 + i) * NT2:(pu * 8 + i + 1) * NT2]

                prev_fb = None
                tile_silu = {}
                for p in range(NPAIR):
                    parity = p % 2
                    is_real = (2 * p) < 16

                    def emit_front(p, u, parity, is_real):
                        pu = u % 2
                        early = (p == 0 and u < 2)
                        if is_real:
                            wcb = [(s_v, combine_done_tok), (s_w, tok_onehot),
                                   silu_tok[3], (s_v, cbc_evac_tok[pu]),
                                   (s_v, cbc_evac_last[0])]
                            pd.op("tensor", lambda te, p=p, u=u: te.matmul(
                                psU[3][:, 0:NT2],
                                onehot_sb[:, (2 * p) * 128:(2 * p + 1) * 128],
                                combine_fm[0:16, u * NT2:(u + 1) * NT2],
                                start=True, stop=False), waits=wcb)
                            pd.op("tensor", lambda te, p=p, u=u: te.matmul(
                                psU[3][:, NT2:2 * NT2],
                                onehot_sb[:, (2 * p + 1) * 128:(2 * p + 2) * 128],
                                combine_fm[0:16, u * NT2:(u + 1) * NT2],
                                start=False, stop=True), incs=[(s_pe, 1)])
                            cbc_tok = s_pe.bump()
                            pd.op("vector", lambda v, pu=pu: v.tensor_copy(
                                cbc_bf[:, pu * 2 * NT2:(pu + 1) * 2 * NT2], psU[3][:, :]),
                                waits=[(s_pe, cbc_tok)], incs=[(s_v, 1)], drain=True)
                            cbc_evac_tok[pu] = s_v.bump()
                            cbc_evac_last[0] = cbc_evac_tok[pu]

                        if early:
                            for b in range(4):
                                silu_tok[b] = early_silu[(u, b)]
                        for e01 in range(2 if not early else 0):
                            for ip in range(2):
                                b = 2 * e01 + ip
                                wup_w = [(s_wp[parity], wpair_tok[p]), (s_v, xhat2_tok),
                                         silu_tok[b]]
                                if b == 3 and is_real:
                                    wup_w.append((s_v, cbc_evac_tok[pu]))
                                for i2 in range(2):
                                    i = 2 * ip + i2
                                    for k in range(DC):
                                        first = (i2 == 0 and k == 0)
                                        last = (i2 == 1 and k == DC - 1)
                                        inc = [(s_pe, 1)] if last else []
                                        pd.op("tensor",
                                              lambda te, e01=e01, i=i, k=k, i2=i2, b=b,
                                              parity=parity, u=u, first=first, last=last:
                                              te.matmul(
                                                  psU[b][:, i2 * NT2:(i2 + 1) * NT2],
                                                  wup[parity][:, (e01 * DC + k) * 512 + i * 128:
                                                              (e01 * DC + k) * 512 + (i + 1) * 128],
                                                  xhat2[:, k * T + u * NT2:k * T + (u + 1) * NT2],
                                                  start=first, stop=last),
                                              waits=wup_w if first else [], incs=inc)
                                up_bank_tok[b] = s_pe.bump()

                        nonlocal_last = [0]
                        for b in range(4 if not early else 0):
                            e01, ip = b // 2, b % 2
                            wsl = [(s_pe, up_bank_tok[b])]
                            if b == 0 and last_mult_tok:
                                wsl.append((s_v, last_mult_tok))
                            if is_real:
                                pd.op("scalar", lambda sc, e01=e01, ip=ip, b=b, pu=pu:
                                      sc.activation(
                                          silu_sb[:, (pu * 8 + e01 * 4 + 2 * ip) * NT2:
                                                  (pu * 8 + e01 * 4 + 2 * ip + 2) * NT2],
                                          psU[b][:, :], AF.Silu),
                                      waits=wsl, incs=[(s_a, 1)])
                            else:
                                pd.op("scalar", lambda sc, e01=e01, ip=ip, b=b, pu=pu:
                                      sc.activation(
                                          scaled_sb[:, (pu * 8 + e01 * 4 + 2 * ip) * NT2:
                                                    (pu * 8 + e01 * 4 + 2 * ip + 2) * NT2],
                                          psU[b][:, :], AF.Silu),
                                      waits=wsl + ([(s_pe, down_pe_tok[pu])]
                                                   if (b == 0 and down_pe_tok[pu]) else []),
                                      incs=[(s_a, 1)])
                            silu_tok[b] = (s_a, s_a.bump())

                        if is_real:
                            for e01 in range(2):
                                for ip in range(2):
                                    b = 2 * e01 + ip
                                    last = (e01 == 1 and ip == 1)
                                    inc = [(s_v, 1)] if last else []
                                    wml = [silu_tok[b]]
                                    if b == 0 and down_pe_tok[pu]:
                                        wml.append((s_pe, down_pe_tok[pu]))
                                    pd.op("vector", lambda v, e01=e01, ip=ip, pu=pu: v.tensor_tensor(
                                        scaled_sb[:, (pu * 8 + e01 * 4 + 2 * ip) * NT2:
                                                  (pu * 8 + e01 * 4 + 2 * ip + 2) * NT2].rearrange(
                                                      "p (i t) -> p i t", i=2),
                                        silu_sb[:, (pu * 8 + e01 * 4 + 2 * ip) * NT2:
                                                (pu * 8 + e01 * 4 + 2 * ip + 2) * NT2]
                                        .rearrange("p (i t) -> p i t", i=2),
                                        cbc_bf[:, (pu * 2 + e01) * NT2:(pu * 2 + e01 + 1) * NT2]
                                        .rearrange("p (o t) -> p o t", o=1)
                                        .to_broadcast([128, 2, NT2]),
                                        op=ALU.mult), waits=wml, incs=inc,
                                        drain=(e01 == 0 and ip == 0))
                            scaled_tok[pu] = s_v.bump()
                        else:
                            scaled_tok[pu] = 0

                        tile_silu[(p, u)] = (silu_tok[1], silu_tok[3])
                        return silu_tok[3]

                    def emit_back_tail(p, u, parity, is_real):
                        pu = u % 2
                        first_w = [(s_v, moe_evac_tok)] if moe_evac_tok else []
                        first_w.append(tile_silu[(p, u)][1])
                        bank_tok = [0] * 4
                        for hp in range(4):
                            for e01 in range(2):
                                for hh in range(2):
                                    h = 2 * hp + hh
                                    for i in range(4):
                                        first_all = (hp == 0 and e01 == 0
                                                     and hh == 0 and i == 0)
                                        last_b = (e01 == 1 and hh == 1 and i == 3)
                                        inc = [(s_pe, 1)] if last_b else []
                                        pd.op("tensor",
                                              lambda te, e01=e01, h=h, i=i, hp=hp,
                                              parity=parity, u=u, pu=pu,
                                              e0=(e01 == 0 and hh == 0 and i == 0),
                                              st=last_b: te.matmul(
                                                  psM[hp][:, (h % 2) * NT2:
                                                          (h % 2 + 1) * NT2],
                                                  wdn[parity][:, (e01 * 4 + i) * D
                                                              + h * 128:
                                                              (e01 * 4 + i) * D
                                                              + (h + 1) * 128],
                                                  scaled_sb[:, (pu * 8 + e01 * 4 + i)
                                                            * NT2:
                                                            (pu * 8 + e01 * 4 + i + 1)
                                                            * NT2],
                                                  start=e0, stop=st),
                                              waits=first_w if first_all else [],
                                              incs=inc)
                            bank_tok[hp] = s_pe.bump()
                            for hh in range(2):
                                h = 2 * hp + hh
                                pd.op("vector", lambda v, h=h, u=u, hp=hp: v.tensor_add(
                                    x1[:, US2(h, u)], x1[:, US2(h, u)],
                                    psM[hp][:, (h % 2) * NT2:(h % 2 + 1) * NT2]),
                                    waits=[(s_pe, bank_tok[hp])] if hh == 0 else [],
                                    incs=[(s_v, 1)], drain=(hh == 0))
                                htok = s_v.bump()
                                pd.op("sync", lambda sy, u=u, h=h: sy.dma_start(
                                    out_r[:, h, u * NT2:(u + 1) * NT2],
                                    x1v[:, h, u * NT2:(u + 1) * NT2]),
                                    waits=[(s_v, htok)], incs=[(s_o, 16)])
                                s_o.bump(16)
                        return bank_tok[3], s_v.n

                    def emit_back(p, u, parity, is_real):
                        nonlocal_moe = [0]
                        pu = u % 2
                        for e01 in range(2):
                            for h in range(DC):
                                bm = psM[h // 2]
                                off = (h % 2) * NT2
                                for i in range(4):
                                    first_all = (e01 == 0 and h == 0 and i == 0)
                                    last_all = (e01 == 1 and h == DC - 1 and i == 3)
                                    wdn_w = []
                                    if first_all:
                                        wdn_w = [(s_v, moe_evac_tok)] if moe_evac_tok else []
                                        if is_real:
                                            wdn_w.append((s_v, scaled_tok[pu]))
                                        else:
                                            wdn_w.append(tile_silu[(p, u)][1])
                                    inc = [(s_pe, 1)] if last_all else []
                                    pd.op("tensor",
                                          lambda te, e01=e01, h=h, i=i, bm=bm, off=off,
                                          parity=parity, u=u, pu=pu: te.matmul(
                                              bm[:, off:off + NT2],
                                              wdn[parity][:, (e01 * 4 + i) * D + h * 128:
                                                          (e01 * 4 + i) * D + (h + 1) * 128],
                                              scaled_sb[:, (pu * 8 + e01 * 4 + i) * NT2:
                                                        (pu * 8 + e01 * 4 + i + 1) * NT2],
                                              start=(e01 == 0 and i == 0 and h % 2 == 0),
                                              stop=(e01 == 1 and i == 3 and h % 2 == 1)),
                                          waits=wdn_w if first_all else [], incs=inc)
                        dtok = s_pe.bump()
                        down_pe_tok[pu] = dtok

                        last_pair = (p == NPAIR - 1)
                        for h in range(DC):
                            inc = [(s_v, 1)] if (last_pair or h == DC - 1) else []
                            wev = [(s_pe, dtok)] if h == 0 else []
                            if h == 0 and p == 0 and u == 0 and DEBUG_TAPS:
                                wev.append((s_dbg2, 16))
                            pd.op("vector", lambda v, h=h, u=u: v.tensor_add(
                                x1[:, US2(h, u)], x1[:, US2(h, u)],
                                psM[h // 2][:, (h % 2) * NT2:(h % 2 + 1) * NT2]),
                                waits=wev, incs=inc, drain=(h == 0))
                            if last_pair:
                                htok = s_v.bump()
                                pd.op("sync", lambda sy, u=u, h=h: sy.dma_start(
                                    out_r[:, h, u * NT2:(u + 1) * NT2],
                                    x1v[:, h, u * NT2:(u + 1) * NT2]),
                                    waits=[(s_v, htok)], incs=[(s_o, 16)])
                                s_o.bump(16)
                        mtok = s_v.n if last_pair else s_v.bump()
                        return dtok, mtok

                    for u in range(NTILE2):
                        emit_front(p, u, parity, is_real)
                        last_mult_tok = scaled_tok[u % 2] or last_mult_tok
                        prev = (p, u - 1, parity, is_real) if u >= 1 else prev_fb
                        if prev is not None:
                            down_tok, moe_evac_tok = emit_back(*prev)
                            if prev[1] == NTILE2 - 1 and prev[0] + 2 < NPAIR:
                                emit_pair_dma(pd, prev[0] + 2,
                                              waits=[(s_pe, down_tok)])
                    prev_fb = (p, NTILE2 - 1, parity, is_real)
                if prev_fb is not None:
                    p, u, parity, is_real = prev_fb
                    down_tok, moe_evac_tok = emit_back_tail(p, u, parity, is_real)

                wend = [(s_o, s_o.n)]
                if DEBUG_TAPS:
                    wend.append((s_dbg, s_dbg.n))
                    wend.append((s_dbg2, s_dbg2.n))
                pd.op("sync", lambda sy: sy.nop(), waits=wend)

                with nc.Block("ffn_pairs") as blk:
                    pd.replay(blk)

    return nc


def kernel(L0_attn_x, L0_attn_w_qkva, L0_attn_gamma, L0_ffn_gate_weight,
           L0_ffn_expert_up, L0_ffn_expert_down, L0_ffn_shared_up, L0_ffn_shared_down):
    x = np.asarray(L0_attn_x, dtype=np.float32)
    B = x.shape[0]
    wqT = np.ascontiguousarray(np.asarray(L0_attn_w_qkva, np.float32).T)
    gamma_pc = np.ascontiguousarray(np.asarray(L0_attn_gamma, np.float32).reshape(DC, 128).T)
    gateT = np.ascontiguousarray(np.asarray(L0_ffn_gate_weight, np.float32).T)
    eup = np.ascontiguousarray(np.asarray(L0_ffn_expert_up, np.float32))
    edn = np.ascontiguousarray(np.asarray(L0_ffn_expert_down, np.float32))
    sup = np.ascontiguousarray(np.asarray(L0_ffn_shared_up, np.float32))
    sdn = np.ascontiguousarray(np.asarray(L0_ffn_shared_down, np.float32))
    identc = np.eye(128, dtype=np.float32)
    onehotc = np.zeros((16, 16 * 128), dtype=np.float32)
    for e in range(16):
        onehotc[e, e * 128:(e + 1) * 128] = 1

    nc = build_program()
    in_maps = []
    for b in range(B):
        in_maps.append(dict(
            xt=np.ascontiguousarray(x[b].T), wqT=wqT, gamma_pc=gamma_pc, gateT=gateT,
            eup=eup, edn=edn, sup=sup, sdn=sdn, identc=identc, onehotc=onehotc))
    res = run_bass_kernel_spmd(nc, in_maps, core_ids=list(range(B)))
    outs = [np.ascontiguousarray(r["out"].T) for r in res.results]
    return np.stack(outs, axis=0)


if __name__ == "__main__":
    import reference as _r
    ins = _r.setup_inputs()
    got = kernel(**{k: np.asarray(v) for k, v in ins.items()})
    ref = np.asarray(_r.reference(**ins))
    err = np.linalg.norm(got - ref) / np.linalg.norm(ref)
    print("rel_err:", err)


# version 35
# speedup vs baseline: 1.0135x; 1.0135x over previous
import numpy as np
import ml_dtypes
from contextlib import ExitStack

import concourse.bass as bass
import concourse.mybir as mybir
from concourse.bass_utils import run_bass_kernel_spmd

F32 = mybir.dt.float32
BF16 = mybir.dt.bfloat16
AF = mybir.ActivationFunctionType
ALU = mybir.AluOpType
AX = mybir.AxisListType

T = 2048
D = 1024
DC = 8
M3 = 3072
NT1 = 512
NTILE1 = T // NT1
NT2 = 256
NTILE2 = T // NT2
NE = 20
NPAIR = NE // 2
EPS = 1e-5


class Sem:
    def __init__(self, h):
        self.h = h
        self.n = 0

    def bump(self, k=1):
        self.n += k
        return self.n


class Prog:
    ENGINES = ("sync", "gpsimd", "tensor", "vector", "scalar")

    def __init__(self):
        self.q = {e: [] for e in self.ENGINES}
        self.lastwait = {e: {} for e in self.ENGINES}

    def op(self, eng, fn, waits=(), incs=(), drain=False, lp=False):
        ws = []
        for (s, v) in waits:
            if v <= 0:
                continue
            if v > self.lastwait[eng].get(id(s), 0):
                ws.append((s, v))
                self.lastwait[eng][id(s)] = v
        self.q[eng].append((ws, fn, list(incs), drain, lp))

    def replay(self, block):
        for eng in self.ENGINES:
            ops = self.q[eng]
            if not ops:
                continue

            def run(e, ops=ops):
                for ws, fn, incs, drain, lp in ops:
                    if drain:
                        e.drain()
                    for s, v in ws:
                        e.wait_ge(s.h, v)
                    if lp:
                        with e.bass.allow_low_precision(reason="intentional bf16"):
                            ins = fn(e)
                    else:
                        ins = fn(e)
                    for s, k in incs:
                        ins.then_inc(s.h, k)

            getattr(block, eng)(run)


DEBUG_TAPS = False


def TS1(c, t):
    return slice(c * T + t * NT1, c * T + (t + 1) * NT1)


def US2(c, u):
    return slice(c * T + u * NT2, c * T + (u + 1) * NT2)


def build_program():
    nc = bass.Bass("TRN2", target_bir_lowering=False)

    xt = nc.dram_tensor("xt", [D, T], F32, kind="ExternalInput")
    wqT = nc.dram_tensor("wqT", [D, M3], F32, kind="ExternalInput")
    gamma_pc = nc.dram_tensor("gamma_pc", [128, DC], F32, kind="ExternalInput")
    gateT = nc.dram_tensor("gateT", [D, 16], F32, kind="ExternalInput")
    eup = nc.dram_tensor("eup", [16, D, 512], F32, kind="ExternalInput")
    edn = nc.dram_tensor("edn", [16, 512, D], F32, kind="ExternalInput")
    sup = nc.dram_tensor("sup", [D, 2048], F32, kind="ExternalInput")
    sdn = nc.dram_tensor("sdn", [2048, D], F32, kind="ExternalInput")
    identc = nc.dram_tensor("identc", [128, 128], F32, kind="ExternalInput")
    onehotc = nc.dram_tensor("onehotc", [16, 16 * 128], F32, kind="ExternalInput")
    out = nc.dram_tensor("out", [D, T], F32, kind="ExternalOutput")
    if DEBUG_TAPS:
        dbg_x0 = nc.dram_tensor("dbg_x0", [128, DC * T], F32, kind="ExternalOutput")
        dbg_x1 = nc.dram_tensor("dbg_x1", [128, DC * T], F32, kind="ExternalOutput")
        dbg_xh2 = nc.dram_tensor("dbg_xh2", [128, DC * T], F32, kind="ExternalOutput")
        dbg_cfm = nc.dram_tensor("dbg_cfm", [16, T], F32, kind="ExternalOutput")
        dbg_scaled0 = nc.dram_tensor("dbg_scaled0", [128, 2 * 4 * NT2], F32, kind="ExternalOutput")
        dbg_cbc0 = nc.dram_tensor("dbg_cbc0", [128, 2 * NT2], F32, kind="ExternalOutput")
        dbg_silu0 = nc.dram_tensor("dbg_silu0", [128, 2 * 4 * NT2], F32, kind="ExternalOutput")
        dbg_x1p = [nc.dram_tensor(f"dbg_x1p{i}", [128, DC * NT2], F32,
                                  kind="ExternalOutput") for i in range(NPAIR)]

    xt_r = xt.rearrange("(c p) t -> p c t", p=128)
    wqT_r = wqT.rearrange("(c p) m -> p c m", p=128)
    gateT_r = gateT.rearrange("(c p) e -> p c e", p=128)
    sup_r = sup.rearrange("(c p) i -> p c i", p=128)
    eup_r = eup.rearrange("e (c p) i -> e p c i", p=128)
    edn_r = edn.rearrange("e (c p) h -> e p c h", p=128)
    out_r = out.rearrange("(c p) t -> p c t", p=128)

    with ExitStack() as persist:
        en = persist.enter_context
        x1 = en(nc.sbuf_tensor("x1", [128, DC * T], F32))
        gamma_sb = en(nc.sbuf_tensor("gamma_sb", [128, DC], F32))
        gate_bf = en(nc.sbuf_tensor("gate_bf", [128, DC * 16], BF16))
        ones_col = en(nc.sbuf_tensor("ones_col", [128, 1], BF16))
        ones_colf = en(nc.sbuf_tensor("ones_colf", [128, 1], F32))
        ones_rb = en(nc.sbuf_tensor("ones_rb", [1, 128], BF16))
        carry = en(nc.sbuf_tensor("carry", [128, DC], F32))
        rows = en(nc.sbuf_tensor("rows", [1, 2 * NT1], F32))
        rows_bf = en(nc.sbuf_tensor("rows_bf", [1, 4 * NT1], BF16))
        combine_fm = en(nc.sbuf_tensor("combine_fm", [16, T], BF16))
        crows = en(nc.sbuf_tensor("crows", [1, 8 * NT1], BF16))
        ph_ps = ExitStack()

        s_in = Sem(en(nc.semaphore("s_in")))
        s_w = Sem(en(nc.semaphore("s_w")))
        s_pe = Sem(en(nc.semaphore("s_pe")))
        s_v = Sem(en(nc.semaphore("s_v")))
        s_a = Sem(en(nc.semaphore("s_a")))
        s_o = Sem(en(nc.semaphore("s_o")))
        s_wp = [Sem(en(nc.semaphore("s_wp0"))), Sem(en(nc.semaphore("s_wp1")))]
        s_xt = [Sem(en(nc.semaphore(f"s_xt{i}"))) for i in range(NTILE1)]
        s_wm = [Sem(en(nc.semaphore(f"s_wm{i}"))) for i in range(3)]
        s_dbg = Sem(en(nc.semaphore("s_dbg"))) if DEBUG_TAPS else None
        s_dbg2 = Sem(en(nc.semaphore("s_dbg2"))) if DEBUG_TAPS else None

        def row(i, n=NT1):
            return rows[0:1, i * NT1:i * NT1 + n]

        def rowb(i, n=NT1):
            return rows_bf[0:1, i * NT1:i * NT1 + n]

        x1v = x1.rearrange("p (c t) -> p c t", c=DC)

        with ExitStack() as phb:
            eb = phb.enter_context
            wq_bf = eb(nc.sbuf_tensor("wq_bf", [128, DC * M3], BF16))
            xhatB = eb(nc.sbuf_tensor("xhatB", [128, 2 * DC * NT1], BF16))
            xst = eb(nc.sbuf_tensor("xst", [128, 2 * DC * NT1], BF16))
            q_bf = eb(nc.sbuf_tensor("q_bf", [128, DC * NT1], BF16))
            kvb = eb(nc.sbuf_tensor("kvb", [128, DC * NT1], BF16))
            yb = eb(nc.sbuf_tensor("yb", [128, DC * NT1], BF16))
            ab = eb(nc.sbuf_tensor("ab", [128, DC * NT1], F32))
            psS = [eb(nc.psum_tensor(f"psS{i}", [128, NT1], F32)) for i in range(2)]
            psQ = [eb(nc.psum_tensor(f"psQ{i}", [128, NT1], F32)) for i in range(4)]
            psBC = [eb(nc.psum_tensor(f"psBC{i}", [128, NT1], F32)) for i in range(2)]

            pb = Prog()
            pa = pb
            pa.op("sync", lambda sy: sy.dma_start(gamma_sb[:, :], gamma_pc[:, :]),
                  incs=[(s_in, 16)])
            tok_gamma = s_in.bump(16)
            for t in range(NTILE1):
                for c in range(DC):
                    eng = "sync" if c % 2 == 0 else "scalar"
                    pa.op(eng, lambda e, c=c, t=t: e.dma_start(
                        x1[:, TS1(c, t)], xt_r[:, c, t * NT1:(t + 1) * NT1]),
                        incs=[(s_xt[t], 16)])
                    s_xt[t].bump(16)
    
            MG = 4 * 128
            wq_mstarts = [8, 12, 16, 20, 0, 4]
            for g6, ms in enumerate(wq_mstarts):
                sem = s_wm[g6 % 2]
                for c in range(DC):
                    wqw = []
                    if c == 0:
                        if g6 < 2:
                            wqw = [(s_xt[0], s_xt[0].n)]
                        else:
                            wqw = [(sem, 128 * (g6 // 2))]
                    pa.op("gpsimd", lambda g, c=c, ms=ms: g.dma_start(
                        wq_bf[:, c * M3 + ms * 128:c * M3 + ms * 128 + MG],
                        wqT_r[:, c, ms * 128:ms * 128 + MG]),
                        waits=wqw, incs=[(sem, 16)])
                    sem.bump(16)
            pa.op("gpsimd", lambda g: g.dma_start(
                gate_bf.rearrange("p (c e) -> p c e", c=DC), gateT_r[:, :, :]),
                incs=[(s_w, 16)])
            tok_gate = s_w.bump(16)

            pa.op("vector", lambda v: v.memset(ones_col[:, :], 1.0))
            pa.op("vector", lambda v: v.memset(ones_colf[:, :], 1.0))
            pa.op("vector", lambda v: v.memset(ones_rb[:, :], 1.0))
            pa.op("vector", lambda v: v.memset(carry[:, :], 0.0), incs=[(s_v, 1)])
            tok_init = s_v.bump()

            if DEBUG_TAPS:
                pb.op("gpsimd", lambda g: g.dma_start(dbg_x0[:, :], x1[:, :]),
                      waits=[(s_in, tok_x)], incs=[(s_dbg, 16)])
                s_dbg.bump(16)

            cast_tok = [0] * NTILE1
            stats_tok = [0] * NTILE1
            rows_tok = [0] * NTILE1
            rowsA_tok = [0] * NTILE1
            xhat_tok = [0] * NTILE1
            scan_tok = [0] * NTILE1
            bcN_tok = [0] * NTILE1
            bcA_tok = [0] * NTILE1
            qm_tok = {}
            ev_tok = {}
            x1d_tok = [0] * NTILE1

            def XS(c, par):
                return xst[:, (par * DC + c) * NT1:(par * DC + c + 1) * NT1]

            def XH(c, par):
                return xhatB[:, (par * DC + c) * NT1:(par * DC + c + 1) * NT1]

            psfree_tok = [0] * NTILE1
            sx_tok = [0] * NTILE1
            sq_tok = [0] * NTILE1

            def emit_cast(t):
                par = t % 2
                w = [(s_xt[t], s_xt[t].n)]
                if t == 0:
                    w.append((s_v, tok_init))
                if t >= 2:
                    w.append((s_pe, stats_tok[t - 2]))
                for c in range(DC):
                    inc = [(s_a, 1)] if c == DC - 1 else []
                    pb.op("scalar", lambda sc, c=c, t=t, par=par: sc.copy(
                        XS(c, par), x1[:, TS1(c, t)]),
                        waits=w if c == 0 else [], incs=inc)
                cast_tok[t] = s_a.bump()

            def emit_sum(t):
                par = t % 2
                w0 = [(s_a, cast_tok[t])]
                if t >= 2:
                    w0.append((s_v, psfree_tok[t - 2]))
                for c in range(DC):
                    inc = [(s_pe, 1)] if c == DC - 1 else []
                    pb.op("tensor", lambda te, c=c, par=par: te.matmul(
                        psS[par][0:1, :], ones_col[:, :], XS(c, par),
                        start=(c == 0), stop=(c == DC - 1)),
                        waits=w0 if c == 0 else [], incs=inc)
                sx_tok[t] = s_pe.bump()
                wng = [(s_pe, sx_tok[t])]
                if t >= 2:
                    wng.append((s_pe, bcN_tok[t - 2]))
                pb.op("vector", lambda v, par=par: v.tensor_single_scalar(
                    rowb(par), psS[par][0:1, :], -1.0 / D, op=ALU.mult),
                    waits=wng, drain=True, incs=[(s_v, 1)])
                rows_tok[t] = s_v.bump()

            def emit_sq(t):
                par = t % 2
                for c in range(DC):
                    inc = [(s_a, 1)] if c == DC - 1 else []
                    pb.op("scalar", lambda sc, c=c, par=par: sc.activation(
                        XS(c, par), XS(c, par), AF.Square),
                        waits=[(s_pe, sx_tok[t])] if c == 0 else [],
                        incs=inc, drain=(c == 0))
                sq_tok[t] = s_a.bump()

            def emit_sumsq(t):
                par = t % 2
                for c in range(DC):
                    inc = [(s_pe, 1)] if c == DC - 1 else []
                    pb.op("tensor", lambda te, c=c, par=par: te.matmul(
                        psS[par][32:33, :], ones_col[:, :], XS(c, par),
                        start=(c == 0), stop=(c == DC - 1), tile_position=(0, 32)),
                        waits=[(s_a, sq_tok[t])] if c == 0 else [], incs=inc)
                stats_tok[t] = s_pe.bump()

            def emit_var(t):
                par = t % 2
                pb.op("vector", lambda v, par=par: v.tensor_single_scalar(
                    row(0), psS[par][32:33, :], 1.0 / D, op=ALU.mult),
                    waits=[(s_pe, stats_tok[t])], drain=True, incs=[(s_v, 1)])
                psfree_tok[t] = s_v.bump()
                pb.op("vector", lambda v, par=par: v.tensor_mul(
                    row(1), rowb(par), rowb(par)), drain=True)
                pb.op("vector", lambda v: v.tensor_sub(
                    row(0), row(0), row(1)), drain=True, incs=[(s_v, 1)])
                var_tok = s_v.bump()
                pb.op("scalar", lambda sc: sc.activation(row(1), row(0), AF.Sqrt),
                      waits=[(s_v, var_tok)], incs=[(s_a, 1)])
                sv_tok = s_a.bump()
                pb.op("vector", lambda v: v.reciprocal(row(0), row(1)),
                      waits=[(s_a, sv_tok)])
                wcp = []
                if t >= 2:
                    wcp.append((s_pe, bcA_tok[t - 2]))
                pb.op("vector", lambda v, par=par: v.tensor_copy(rowb(2 + par), row(0)),
                      waits=wcp, incs=[(s_v, 1)], drain=True)
                rowsA_tok[t] = s_v.bump()

            def emit_bc(t):
                par = t % 2
                w = [(s_v, rows_tok[t])]
                if t >= 1:
                    w.append((s_v, xhat_tok[t - 1]))
                pb.op("tensor", lambda te, par=par: te.matmul(
                    psBC[0][:, :], ones_rb[0:1, :], rowb(par), start=True, stop=True),
                    waits=w, incs=[(s_pe, 1)])
                bcN_tok[t] = s_pe.bump()
                pb.op("tensor", lambda te, par=par: te.matmul(
                    psBC[1][:, :], ones_rb[0:1, :], rowb(2 + par), start=True, stop=True),
                    waits=[(s_v, rowsA_tok[t])], incs=[(s_pe, 1)])
                bcA_tok[t] = s_pe.bump()

            def emit_xhat(t):
                par = t % 2
                xh_w = [(s_in, tok_gamma), (s_pe, bcN_tok[t])]
                if t >= 2:
                    xh_w.append((s_pe, qm_tok[(t - 2, 7)]))
                for c in range(DC):
                    pb.op("vector", lambda v, c=c, t=t, par=par: v.tensor_tensor(
                        XH(c, par), x1[:, TS1(c, t)], psBC[0][:, :], op=ALU.add),
                        waits=xh_w if c == 0 else [], drain=(c == 0))
                for c in range(DC):
                    inc = [(s_v, 1)] if c == DC - 1 else []
                    wx2 = [(s_pe, bcA_tok[t])] if c == 0 else []
                    pb.op("vector", lambda v, c=c, par=par: v.scalar_tensor_tensor(
                        XH(c, par), XH(c, par), gamma_sb[:, c:c + 1],
                        psBC[1][:, :], op0=ALU.mult, op1=ALU.mult),
                        waits=wx2, incs=inc, drain=(c == 0))
                xhat_tok[t] = s_v.bump()

            def emit_qkva_scan(t, mids=None):
                par = t % 2
                ms_order = list(range(8, 24)) + list(range(0, 8))
                for pi, m in enumerate(ms_order):
                    if mids and pi in mids:
                        mids[pi]()
                    bank = psQ[pi % 4]
                    wq_w = []
                    if pi == 0:
                        wq_w = [(s_v, xhat_tok[t])]
                    if t == 0 and pi % 4 == 0:
                        g6 = pi // 4
                        wq_w.append((s_wm[g6 % 2], 128 * (g6 // 2 + 1)))
                    key = ((t, ms_order[pi - 4]) if pi >= 4 else
                           ((t - 1, ms_order[pi + 20]) if t > 0 else None))
                    if key is not None and key in ev_tok:
                        wq_w.append(ev_tok[key])
                    for k in range(DC):
                        inc = [(s_pe, 1)] if k == DC - 1 else []
                        pb.op("tensor", lambda te, k=k, m=m, bank=bank, par=par: te.matmul(
                            bank[:, :],
                            wq_bf[:, k * M3 + m * 128:k * M3 + (m + 1) * 128],
                            XH(k, par), start=(k == 0), stop=(k == DC - 1)),
                            waits=wq_w if k == 0 else [], incs=inc)
                    qm_tok[(t, m)] = s_pe.bump()
                    if m < 8:
                        wqe = [(s_pe, qm_tok[(t, m)])]
                        if m == 0 and t > 0:
                            wqe.append((s_v, x1d_tok[t - 1]))
                        pb.op("scalar", lambda sc, m=m, bank=bank: sc.copy(
                            q_bf[:, m * NT1:(m + 1) * NT1], bank[:, :]),
                            waits=wqe, incs=[(s_a, 1)])
                        ev_tok[(t, m)] = (s_a, s_a.bump())
                    elif m < 16:
                        c = m - 8
                        wkv = [(s_pe, qm_tok[(t, m)])]
                        if m == 8 and t > 0:
                            wkv.append((s_v, scan_tok[t - 1]))
                        pb.op("scalar", lambda sc, c=c, bank=bank: sc.copy(
                            kvb[:, c * NT1:(c + 1) * NT1], bank[:, :]),
                            waits=wkv, incs=[(s_a, 1)])
                        ev_tok[(t, m)] = (s_a, s_a.bump())
                    else:
                        c = m - 16
                        wa = [(s_pe, qm_tok[(t, m)])]
                        if m == 16 and t > 0:
                            wa.append((s_v, x1d_tok[t - 1]))
                        pb.op("scalar", lambda sc, c=c, bank=bank: sc.activation(
                            ab[:, c * NT1:(c + 1) * NT1], bank[:, :], AF.Sigmoid),
                            waits=wa, incs=[(s_a, 1)],
                            drain=(m == 16))
                    
                        ev_tok[(t, m)] = (s_a, s_a.bump())
                a_done_tok = ev_tok[(t, 23)][1]
                q_done_tok = ev_tok[(t, 7)][1]

                for c in range(DC):
                    inc = [(s_v, 1)] if c == DC - 1 else []
                    pb.op("vector", lambda v, c=c: v.tensor_tensor_scan(
                        yb[:, c * NT1:(c + 1) * NT1], ab[:, c * NT1:(c + 1) * NT1],
                        kvb[:, c * NT1:(c + 1) * NT1], carry[:, c:c + 1],
                        op0=ALU.mult, op1=ALU.add),
                        waits=[(s_a, a_done_tok)] if c == 0 else [],
                        incs=inc, drain=(c == 0))
                scan_tok[t] = s_v.bump()
                pb.op("vector", lambda v: v.tensor_copy(
                    carry.rearrange("p (c o) -> p c o", o=1),
                    yb.rearrange("p (c t) -> p c t", c=DC)[:, :, NT1 - 1:NT1]),
                    drain=True)
                for c in range(DC):
                    pb.op("vector", lambda v, c=c: v.tensor_mul(
                        ab[:, c * NT1:(c + 1) * NT1],
                        q_bf[:, c * NT1:(c + 1) * NT1],
                        yb[:, c * NT1:(c + 1) * NT1]),
                        waits=[(s_a, q_done_tok)] if c == 0 else [], drain=(c == 0))
                for c in range(DC):
                    inc = [(s_v, 1)] if c == DC - 1 else []
                    wres = []
                    if c == 0 and t == 0 and DEBUG_TAPS:
                        wres = [(s_dbg, 16)]
                    pb.op("vector", lambda v, c=c, t=t: v.tensor_add(
                        x1[:, TS1(c, t)], x1[:, TS1(c, t)], ab[:, c * NT1:(c + 1) * NT1]),
                        waits=wres, incs=inc, drain=(c == 0))
                x1d_tok[t] = s_v.bump()
                return x1d_tok[t]

            c_stats_tok = [0] * NTILE1
            c_rows_tok = [0] * NTILE1
            c_rowsA_tok = [0] * NTILE1

            def cneg(j):
                return crows[0:1, (2 * j) * NT1:(2 * j + 1) * NT1]

            def crst(j):
                return crows[0:1, (2 * j + 1) * NT1:(2 * j + 2) * NT1]

            c_psfree_tok = [0] * NTILE1
            c_sx_tok = [0] * NTILE1
            c_sq_tok = [0] * NTILE1
            c_cast_tok = [0] * NTILE1

            def emit_c_sum_f32(j):
                par = j % 2
                w0 = [(s_v, x1d_tok[j]),
                      (s_v, psfree_tok[2 + par] if j < 2 else c_psfree_tok[j - 2])]
                for c in range(DC):
                    inc = [(s_pe, 1)] if c == DC - 1 else []
                    pb.op("tensor", lambda te, c=c, j=j, par=par: te.matmul(
                        psS[par][0:1, :], ones_colf[:, :], x1[:, TS1(c, j)],
                        start=(c == 0), stop=(c == DC - 1)),
                        waits=w0 if c == 0 else [], incs=inc)
                c_sx_tok[j] = s_pe.bump()
                pb.op("vector", lambda v, j=j, par=par: v.tensor_single_scalar(
                    cneg(j), psS[par][0:1, :], -1.0 / D, op=ALU.mult),
                    waits=[(s_pe, c_sx_tok[j])], drain=True, incs=[(s_v, 1)])
                c_rows_tok[j] = s_v.bump()

            def emit_c_sq_f32(j):
                par = j % 2
                w = [(s_v, x1d_tok[j]),
                     (s_pe, stats_tok[2 + par] if j < 2 else c_stats_tok[j - 2])]
                for c in range(DC):
                    inc = [(s_a, 1)] if c == DC - 1 else []
                    pb.op("scalar", lambda sc, c=c, j=j, par=par: sc.activation(
                        XS(c, par), x1[:, TS1(c, j)], AF.Square),
                        waits=w if c == 0 else [], incs=inc, drain=(c == 0))
                c_sq_tok[j] = s_a.bump()

            def emit_c_cast(j):
                par = j % 2
                w = [(s_v, x1d_tok[j])]
                w.append((s_pe, stats_tok[2 + par] if j < 2 else c_stats_tok[j - 2]))
                for c in range(DC):
                    inc = [(s_a, 1)] if c == DC - 1 else []
                    pb.op("scalar", lambda sc, c=c, j=j, par=par: sc.copy(
                        XS(c, par), x1[:, TS1(c, j)]),
                        waits=w if c == 0 else [], incs=inc)
                c_cast_tok[j] = s_a.bump()

            def emit_c_sum(j):
                par = j % 2
                w0 = [(s_a, c_cast_tok[j])]
                w0.append((s_v, psfree_tok[2 + par] if j < 2 else c_psfree_tok[j - 2]))
                for c in range(DC):
                    inc = [(s_pe, 1)] if c == DC - 1 else []
                    pb.op("tensor", lambda te, c=c, par=par: te.matmul(
                        psS[par][0:1, :], ones_col[:, :], XS(c, par),
                        start=(c == 0), stop=(c == DC - 1)),
                        waits=w0 if c == 0 else [], incs=inc)
                c_sx_tok[j] = s_pe.bump()
                pb.op("vector", lambda v, j=j, par=par: v.tensor_single_scalar(
                    cneg(j), psS[par][0:1, :], -1.0 / D, op=ALU.mult),
                    waits=[(s_pe, c_sx_tok[j])], drain=True, incs=[(s_v, 1)])
                c_rows_tok[j] = s_v.bump()

            def emit_c_sq(j):
                par = j % 2
                for c in range(DC):
                    inc = [(s_a, 1)] if c == DC - 1 else []
                    pb.op("scalar", lambda sc, c=c, par=par: sc.activation(
                        XS(c, par), XS(c, par), AF.Square),
                        waits=[(s_pe, c_sx_tok[j])] if c == 0 else [],
                        incs=inc, drain=(c == 0))
                c_sq_tok[j] = s_a.bump()

            def emit_c_sumsq(j):
                par = j % 2
                for c in range(DC):
                    inc = [(s_pe, 1)] if c == DC - 1 else []
                    pb.op("tensor", lambda te, c=c, par=par: te.matmul(
                        psS[par][32:33, :], ones_col[:, :], XS(c, par),
                        start=(c == 0), stop=(c == DC - 1), tile_position=(0, 32)),
                        waits=[(s_a, c_sq_tok[j])] if c == 0 else [], incs=inc)
                c_stats_tok[j] = s_pe.bump()

            def emit_c_var(j):
                par = j % 2
                pb.op("vector", lambda v, par=par: v.tensor_single_scalar(
                    row(0), psS[par][32:33, :], 1.0 / D, op=ALU.mult),
                    waits=[(s_pe, c_stats_tok[j])], drain=True, incs=[(s_v, 1)])
                c_psfree_tok[j] = s_v.bump()
                pb.op("vector", lambda v, j=j: v.tensor_mul(
                    row(1), cneg(j), cneg(j)), drain=True)
                pb.op("vector", lambda v: v.tensor_sub(
                    row(0), row(0), row(1)), drain=True)
                pb.op("vector", lambda v: v.tensor_single_scalar(
                    row(0), row(0), EPS, op=ALU.add), incs=[(s_v, 1)], drain=True)
                vpe_tok = s_v.bump()
                pb.op("scalar", lambda sc: sc.activation(row(1), row(0), AF.Sqrt),
                      waits=[(s_v, vpe_tok)], incs=[(s_a, 1)])
                sd_tok = s_a.bump()
                pb.op("vector", lambda v: v.reciprocal(row(0), row(1)),
                      waits=[(s_a, sd_tok)])
                pb.op("vector", lambda v, j=j: v.tensor_copy(crst(j), row(0)),
                      incs=[(s_v, 1)], drain=True)
                c_rowsA_tok[j] = s_v.bump()


            emit_cast(0)
            emit_sum(0)
            emit_sq(0)
            emit_sumsq(0)
            emit_var(0)
            emit_cast(1)
            emit_sum(1)
            emit_sq(1)
            emit_sumsq(1)
            emit_var(1)
            emit_bc(0)
            emit_xhat(0)
            emit_bc(1)
            emit_xhat(1)
            emit_cast(2)
            emit_qkva_scan(0, mids={
                8: lambda: (emit_cast(3), emit_sum(2), emit_sq(2)),
                16: lambda: (emit_sumsq(2), emit_var(2), emit_sum(3),
                             emit_sq(3))})
            emit_sumsq(3)
            emit_var(3)
            emit_bc(2)
            emit_xhat(2)
            emit_qkva_scan(1, mids={20: lambda: emit_c_cast(0)})
            emit_c_sum(0)
            emit_c_sq(0)
            emit_bc(3)
            emit_xhat(3)
            emit_qkva_scan(2, mids={
                12: lambda: (emit_c_sumsq(0), emit_c_var(0)),
                20: lambda: emit_c_cast(1)})
            emit_c_sum(1)
            emit_c_sq(1)
            x1_done_tok = emit_qkva_scan(3, mids={
                12: lambda: (emit_c_sumsq(1), emit_c_var(1))})
            emit_c_sum_f32(2)
            emit_c_sq_f32(2)
            emit_c_sumsq(2)
            emit_c_var(2)
            emit_c_sum_f32(3)
            emit_c_sq_f32(3)
            emit_c_sumsq(3)
            emit_c_var(3)

            with nc.Block("block1") as blk:
                pb.replay(blk)

        with ExitStack() as phcd:
            ec = phcd.enter_context
            xhat2 = ec(nc.sbuf_tensor("xhat2", [128, DC * T], BF16))
            wup = [ec(nc.sbuf_tensor(f"wup{i}", [128, 2 * DC * 512], BF16)) for i in range(2)]
            wdn = [ec(nc.sbuf_tensor(f"wdn{i}", [128, 2 * 4 * D], BF16)) for i in range(2)]
            wpair_tok = [0] * NPAIR

            def emit_pair_dma(pg, p, waits=()):
                parity = p % 2
                sw = s_wp[parity]
                first = [list(waits)]
                for e01 in range(2):
                    e = 2 * p + e01
                    updst = wup[parity].rearrange("p (e c i) -> p e c i", e=2, c=DC)
                    dndst = wdn[parity].rearrange("p (e c h) -> p e c h", e=2, c=4)
                    if e < 16:
                        pg.op("gpsimd", lambda g, e=e, e01=e01, updst=updst: g.dma_start(
                            updst[:, e01, :, :], eup_r[e]),
                            waits=first[0], incs=[(sw, 16)])
                        first[0] = []
                        sw.bump(16)
                        pg.op("gpsimd", lambda g, e=e, e01=e01, dndst=dndst: g.dma_start(
                            dndst[:, e01, :, :], edn_r[e]), incs=[(sw, 16)])
                        sw.bump(16)
                    else:
                        P_ = e - 16
                        pg.op("gpsimd", lambda g, P_=P_, e01=e01, updst=updst: g.dma_start(
                            updst[:, e01, :, :], sup_r[:, :, P_ * 512:(P_ + 1) * 512]),
                            waits=first[0], incs=[(sw, 16)])
                        first[0] = []
                        sw.bump(16)
                        for c4 in range(4):
                            pg.op("gpsimd", lambda g, P_=P_, e01=e01, c4=c4, parity=parity:
                                  g.dma_start(
                                      wdn[parity][:, (e01 * 4 + c4) * D:(e01 * 4 + c4 + 1) * D],
                                      sdn[P_ * 512 + c4 * 128:P_ * 512 + (c4 + 1) * 128, :]),
                                  incs=[(sw, 16)])
                            sw.bump(16)
                wpair_tok[p] = sw.n

            pc = Prog()
            if DEBUG_TAPS:
                pc.op("vector", lambda v: v.nop(), waits=[(s_dbg, s_dbg.n)])
                pc.op("scalar", lambda sc: sc.nop(), waits=[(s_dbg, s_dbg.n)])
                pc.op("tensor", lambda te: te.nop(), waits=[(s_dbg, s_dbg.n)])
                pc.op("gpsimd", lambda g: g.dma_start(dbg_x1[:, :], x1[:, :]),
                      incs=[(s_dbg2, 16)])
                s_dbg2.bump(16)
            pc.op("gpsimd", lambda g: g.dma_start(ident_sb[:, :], identc[:, :]),
                  incs=[(s_wm[1], 16)])
            tok_ident = s_wm[1].bump(16)
            pc.op("gpsimd", lambda g: g.dma_start(onehot_sb[:, :], onehotc[:, :]),
                  incs=[(s_w, 16)])
            tok_onehot = s_w.bump(16)
            tok_gate = tok_onehot
            emit_pair_dma(pc, 0, waits=[(s_dbg, s_dbg.n)] if DEBUG_TAPS else ())
            emit_pair_dma(pc, 1)

            psU = [ec(nc.psum_tensor(f"psU{i}", [128, 512], F32)) for i in range(4)]
            silu_sb = ec(nc.sbuf_tensor("silu_sb", [128, 2 * 2 * 4 * NT2], BF16))
            onehot_sb = ec(nc.sbuf_tensor("onehot_sb", [16, 16 * 128], BF16))
            early_silu = {}
            psG = ph_ps.enter_context(nc.psum_tensor("psG", [128, 512], F32))
            psTR = ph_ps.enter_context(nc.psum_tensor("psTR", [16, 128], BF16))
            psCB = [ph_ps.enter_context(nc.psum_tensor(f"psCB{i}", [128, NT1], F32))
                    for i in range(2)]
            with ExitStack() as phc:
                ecc = phc.enter_context
                rt = ecc(nc.sbuf_tensor("rt", [128, 768], F32))
                scores_all = ecc(nc.sbuf_tensor("scores_all", [128, 256], F32))
                ident_sb = ecc(nc.sbuf_tensor("ident_sb", [128, 128], BF16))
                mx8_all = ecc(nc.sbuf_tensor("mx8_all", [128, 128], F32))
                comb_all = ecc(nc.sbuf_tensor("comb_all", [128, 256], BF16))

                sc_toks = [0]

                xh2_prev = [0]
                xh2_toks = [0] * NTILE1

                xh2_addprev = [0]

                def emit_cprime(j):
                    wN = [(s_v, xh2_addprev[0])] if xh2_addprev[0] else []
                    pc.op("tensor", lambda te, j=j: te.matmul(
                        psCB[0][:, :], ones_rb[0:1, :], cneg(j),
                        start=True, stop=True), waits=wN, incs=[(s_pe, 1)])
                    cbN = s_pe.bump()
                    wA = [(s_v, xh2_prev[0])] if xh2_prev[0] else []
                    pc.op("tensor", lambda te, j=j: te.matmul(
                        psCB[1][:, :], ones_rb[0:1, :], crst(j),
                        start=True, stop=True), waits=wA, incs=[(s_pe, 1)])
                    cbA = s_pe.bump()
                    HT = NT1 // 2
                    half_tok = [0, 0]
                    for hf in range(2):
                        for c in range(DC):
                            pc.op("vector", lambda v, c=c, j=j, hf=hf: v.tensor_tensor(
                                xhat2[:, c * T + j * NT1 + hf * HT:
                                      c * T + j * NT1 + (hf + 1) * HT],
                                x1[:, c * T + j * NT1 + hf * HT:
                                   c * T + j * NT1 + (hf + 1) * HT],
                                psCB[0][:, hf * HT:(hf + 1) * HT], op=ALU.add),
                                waits=[(s_pe, cbN)] if (c == 0 and hf == 0) else [],
                                drain=(c == 0))
                        for c in range(DC):
                            inc = [(s_v, 1)] if c == DC - 1 else []
                            pc.op("vector", lambda v, c=c, j=j, hf=hf: v.tensor_tensor(
                                xhat2[:, c * T + j * NT1 + hf * HT:
                                      c * T + j * NT1 + (hf + 1) * HT],
                                xhat2[:, c * T + j * NT1 + hf * HT:
                                      c * T + j * NT1 + (hf + 1) * HT],
                                psCB[1][:, hf * HT:(hf + 1) * HT], op=ALU.mult),
                                waits=[(s_pe, cbA)] if (c == 0 and hf == 0) else [],
                                incs=inc, drain=(c == 0))
                        half_tok[hf] = s_v.bump()
                        for s4i in range(2 * hf, 2 * hf + 2):
                            sub = 4 * j + s4i
                            wsc = [(s_v, half_tok[hf]), (s_w, tok_gate)]
                            if sc_toks[0]:
                                wsc.append((s_a, sc_toks[0]))
                            for k in range(DC):
                                inc = [(s_pe, 1)] if k == DC - 1 else []
                                pc.op("tensor", lambda te, k=k, sub=sub: te.matmul(
                                    psG[:, sub * 16:(sub + 1) * 16],
                                    xhat2[:, k * T + sub * 128:k * T + (sub + 1) * 128],
                                    gate_bf[:, k * 16:(k + 1) * 16],
                                    start=(k == 0), stop=(k == DC - 1)),
                                    waits=wsc if k == 0 else [], incs=inc)
                            gm_tok = s_pe.bump()
                            pc.op("scalar", lambda sc, sub=sub: sc.activation(
                                scores_all[:, sub * 16:(sub + 1) * 16],
                                psG[:, sub * 16:(sub + 1) * 16], AF.Sigmoid),
                                waits=[(s_pe, gm_tok)], incs=[(s_a, 1)])
                            sc_toks[0] = s_a.bump()
                    xh2_addprev[0] = half_tok[1]
                    xh2j = half_tok[1]
                    xh2_prev[0] = xh2j
                    xh2_toks[j] = xh2j

                early_upb = {}

                def emit_early_up(u):
                    pu = u % 2
                    for e01 in range(2):
                        for ip in range(2):
                            b = 2 * e01 + ip
                            wup_w = [(s_wp[0], wpair_tok[0]), (s_v, xh2_toks[0])]
                            if (u - 1, b) in early_silu:
                                wup_w.append(early_silu[(u - 1, b)])
                            for i2 in range(2):
                                i = 2 * ip + i2
                                for k in range(DC):
                                    first = (i2 == 0 and k == 0)
                                    last = (i2 == 1 and k == DC - 1)
                                    inc = [(s_pe, 1)] if last else []
                                    pc.op("tensor",
                                          lambda te, e01=e01, i=i, k=k, i2=i2, b=b,
                                          u=u, first=first, last=last:
                                          te.matmul(
                                              psU[b][:, i2 * NT2:(i2 + 1) * NT2],
                                              wup[0][:, (e01 * DC + k) * 512 + i * 128:
                                                     (e01 * DC + k) * 512 + (i + 1) * 128],
                                              xhat2[:, k * T + u * NT2:k * T + (u + 1) * NT2],
                                              start=first, stop=last),
                                          waits=wup_w if first else [], incs=inc)
                            early_upb[(u, b)] = s_pe.bump()

                def emit_early_silu(u):
                    pu = u % 2
                    for b in range(4):
                        e01, ip = b // 2, b % 2
                        pc.op("scalar", lambda sc, e01=e01, ip=ip, b=b, pu=pu, u=u:
                              sc.activation(
                                  silu_sb[:, (pu * 8 + e01 * 4 + 2 * ip) * NT2:
                                          (pu * 8 + e01 * 4 + 2 * ip + 2) * NT2],
                                  psU[b][:, :], AF.Silu),
                              waits=[(s_pe, early_upb[(u, b)])], incs=[(s_a, 1)])
                        early_silu[(u, b)] = (s_a, s_a.bump())

                emit_cprime(0)
                emit_cprime(1)
                emit_cprime(2)
                emit_cprime(3)
                emit_early_up(0)
                sc_tok = sc_toks[0]
                xhat2_tok = s_v.n

                def rv(i0, n):
                    return rt[:, i0:i0 + n]

                def rv3(i0, n, k):
                    return rt[:, i0:i0 + n].rearrange("p (a k) -> p a k", k=k)

                s3 = scores_all.rearrange("p (jg k) -> p jg k", k=4)
                pr = pc
                pr.op("vector", lambda v: v.tensor_add(
                    rv3(0, 128, 2), s3[:, :, 0:2], s3[:, :, 2:4]),
                    waits=[(s_a, sc_tok)], drain=True)
                pr.op("vector", lambda v: v.tensor_add(
                    rv(128, 64), s3[:, :, 0:1], s3[:, :, 1:2]))
                pr.op("vector", lambda v: v.tensor_add(
                    rv(192, 64), s3[:, :, 2:3], s3[:, :, 3:4]))
                pr.op("vector", lambda v: v.tensor_add(
                    rv(256, 64), s3[:, :, 0:1], s3[:, :, 3:4]))
                pr.op("vector", lambda v: v.tensor_add(
                    rv(320, 64), s3[:, :, 1:2], s3[:, :, 2:3]))
                pr.op("vector", lambda v: v.tensor_max(
                    rv(384, 64), rv3(0, 128, 2)[:, :, 0:1], rv3(0, 128, 2)[:, :, 1:2]),
                    drain=True)
                pr.op("vector", lambda v: v.tensor_max(rv(448, 64), rv(128, 64), rv(192, 64)))
                pr.op("vector", lambda v: v.tensor_max(rv(512, 64), rv(256, 64), rv(320, 64)))
                pr.op("vector", lambda v: v.tensor_max(rv(576, 64), rv(384, 64), rv(448, 64)),
                      drain=True)
                pr.op("vector", lambda v: v.tensor_max(rv(640, 64), rv(576, 64), rv(512, 64)),
                      drain=True)
                G4 = rv3(640, 64, 4)
                for gix in range(4):
                    pr.op("vector", lambda v, gix=gix: v.tensor_tensor(
                        rv(gix * 64, 64), G4,
                        G4[:, :, gix:gix + 1].to_broadcast([128, 16, 4]),
                        op=ALU.is_lt), drain=(gix == 0))
                pr.op("vector", lambda v: v.tensor_add(rv(256, 64), rv(0, 64), rv(64, 64)),
                      drain=True)
                pr.op("vector", lambda v: v.tensor_add(rv(320, 64), rv(128, 64), rv(192, 64)))
                pr.op("vector", lambda v: v.tensor_add(rv(256, 64), rv(256, 64), rv(320, 64)),
                      drain=True)
                pr.op("vector", lambda v: v.tensor_single_scalar(
                    rv(320, 64), rv(256, 64), 2.0, op=ALU.is_lt), drain=True)
                pr.op("vector", lambda v: v.tensor_tensor(
                    rv3(0, 256, 4), s3,
                    rv3(320, 64, 1).to_broadcast([128, 64, 4]), op=ALU.mult),
                    drain=True)
                for jj in range(16):
                    pr.op("vector", lambda v, jj=jj: v.max(
                        out=mx8_all[:, jj * 8:(jj + 1) * 8], in_=rv(jj * 16, 16)),
                        drain=(jj == 0))
                th = mx8_all.rearrange("p (j w) -> p j w", w=8)
                ms3 = rv3(0, 256, 16)
                pr.op("vector", lambda v: v.tensor_tensor(
                    rv3(256, 256, 16), ms3,
                    th[:, :, 3:4].to_broadcast([128, 16, 16]), op=ALU.is_ge),
                    drain=True)
                pr.op("vector", lambda v: v.tensor_mul(
                    rv(256, 256), rv(256, 256), rv(0, 256)), drain=True)
                W3 = rv3(256, 256, 16)
                pr.op("vector", lambda v: v.tensor_add(
                    rv3(512, 128, 8), W3[:, :, 0:8], W3[:, :, 8:16]), drain=True)
                d8 = rv3(512, 128, 8)
                pr.op("vector", lambda v: v.tensor_add(
                    rv3(640, 64, 4), d8[:, :, 0:4], d8[:, :, 4:8]), drain=True)
                d4 = rv3(640, 64, 4)
                pr.op("vector", lambda v: v.tensor_add(
                    rv3(704, 32, 2), d4[:, :, 0:2], d4[:, :, 2:4]), drain=True)
                d2 = rv3(704, 32, 2)
                pr.op("vector", lambda v: v.tensor_add(
                    rv(736, 16), d2[:, :, 0:1], d2[:, :, 1:2]), drain=True)
                pr.op("vector", lambda v: v.tensor_single_scalar(
                    rv(736, 16), rv(736, 16), 1e-20, op=ALU.add), drain=True)
                pr.op("vector", lambda v: v.reciprocal(rv(752, 16), rv(736, 16)),
                      drain=True)
                pr.op("vector", lambda v: v.tensor_single_scalar(
                    rv(752, 16), rv(752, 16), 2.5, op=ALU.mult), drain=True)
                pr.op("vector", lambda v: v.tensor_tensor(
                    comb_all.rearrange("p (j e) -> p j e", e=16), W3,
                    rv3(752, 16, 1).to_broadcast([128, 16, 16]), op=ALU.mult),
                    incs=[(s_v, 1)], drain=True)
                comb_tok = s_v.bump()

                emit_early_silu(0)
                emit_early_up(1)
                emit_early_silu(1)

                tr_free = 0
                for jj in range(16):
                    wtr = [(s_v, comb_tok), (s_wm[1], tok_ident)]
                    if tr_free:
                        wtr.append((s_v, tr_free))
                    pc.op("tensor", lambda te, jj=jj: te.transpose(
                        psTR[:, :], comb_all[:, jj * 16:(jj + 1) * 16], ident_sb[:, :]),
                        waits=wtr, incs=[(s_pe, 1)])
                    trm = s_pe.bump()
                    pc.op("vector", lambda v, jj=jj: v.tensor_copy(
                        combine_fm[0:16, jj * 128:(jj + 1) * 128], psTR[:, :]),
                        waits=[(s_pe, trm)], incs=[(s_v, 1)], drain=(jj == 0))
                    tr_free = s_v.bump()
                combine_done_tok = s_v.n

                with nc.Block("ffn_prepass") as blk:
                    pc.replay(blk)
            ph_ps.close()

            pd = Prog()
            if DEBUG_TAPS:
                pd.op("gpsimd", lambda g: g.dma_start(dbg_xh2[:, :], xhat2[:, :]),
                      waits=[(s_v, xhat2_tok), (s_dbg2, s_dbg2.n)], incs=[(s_dbg2, 16)])
                s_dbg2.bump(16)
                pd.op("gpsimd", lambda g: g.dma_start(dbg_cfm[:, :], combine_fm[:, :]),
                      waits=[(s_dbg2, s_dbg2.n)], incs=[(s_dbg2, 16)])
                s_dbg2.bump(16)
            with ExitStack() as phd:
                ed = phd.enter_context
                scaled_sb = ed(nc.sbuf_tensor("scaled_sb", [128, 2 * 8 * NT2], BF16))
                cbc_bf = ed(nc.sbuf_tensor("cbc_bf", [128, 4 * NT2], BF16))
                psM = [ed(nc.psum_tensor(f"psM{i}", [128, 512], F32)) for i in range(4)]

                moe_evac_tok = 0
                silu_tok = {i: (s_a, 0) for i in range(4)}
                cbc_evac_tok = [0, 0]
                cbc_evac_last = [0]
                up_bank_tok = {}
                scaled_tok = [0, 0]
                down_pe_tok = [0, 0]
                last_mult_tok = 0

                def SCL(i, pu):
                    return scaled_sb[:, (pu * 8 + i) * NT2:(pu * 8 + i + 1) * NT2]

                prev_fb = None
                tile_silu = {}
                for p in range(NPAIR):
                    parity = p % 2
                    is_real = (2 * p) < 16

                    def emit_front(p, u, parity, is_real):
                        pu = u % 2
                        early = (p == 0 and u < 2)
                        if is_real:
                            wcb = [(s_v, combine_done_tok), (s_w, tok_onehot),
                                   silu_tok[3], (s_v, cbc_evac_tok[pu]),
                                   (s_v, cbc_evac_last[0])]
                            pd.op("tensor", lambda te, p=p, u=u: te.matmul(
                                psU[3][:, 0:NT2],
                                onehot_sb[:, (2 * p) * 128:(2 * p + 1) * 128],
                                combine_fm[0:16, u * NT2:(u + 1) * NT2],
                                start=True, stop=False), waits=wcb)
                            pd.op("tensor", lambda te, p=p, u=u: te.matmul(
                                psU[3][:, NT2:2 * NT2],
                                onehot_sb[:, (2 * p + 1) * 128:(2 * p + 2) * 128],
                                combine_fm[0:16, u * NT2:(u + 1) * NT2],
                                start=False, stop=True), incs=[(s_pe, 1)])
                            cbc_tok = s_pe.bump()
                            pd.op("vector", lambda v, pu=pu: v.tensor_copy(
                                cbc_bf[:, pu * 2 * NT2:(pu + 1) * 2 * NT2], psU[3][:, :]),
                                waits=[(s_pe, cbc_tok)], incs=[(s_v, 1)], drain=True)
                            cbc_evac_tok[pu] = s_v.bump()
                            cbc_evac_last[0] = cbc_evac_tok[pu]

                        if early:
                            for b in range(4):
                                silu_tok[b] = early_silu[(u, b)]
                        for e01 in range(2 if not early else 0):
                            for ip in range(2):
                                b = 2 * e01 + ip
                                wup_w = [(s_wp[parity], wpair_tok[p]), (s_v, xhat2_tok),
                                         silu_tok[b]]
                                if b == 3 and is_real:
                                    wup_w.append((s_v, cbc_evac_tok[pu]))
                                for i2 in range(2):
                                    i = 2 * ip + i2
                                    for k in range(DC):
                                        first = (i2 == 0 and k == 0)
                                        last = (i2 == 1 and k == DC - 1)
                                        inc = [(s_pe, 1)] if last else []
                                        pd.op("tensor",
                                              lambda te, e01=e01, i=i, k=k, i2=i2, b=b,
                                              parity=parity, u=u, first=first, last=last:
                                              te.matmul(
                                                  psU[b][:, i2 * NT2:(i2 + 1) * NT2],
                                                  wup[parity][:, (e01 * DC + k) * 512 + i * 128:
                                                              (e01 * DC + k) * 512 + (i + 1) * 128],
                                                  xhat2[:, k * T + u * NT2:k * T + (u + 1) * NT2],
                                                  start=first, stop=last),
                                              waits=wup_w if first else [], incs=inc)
                                up_bank_tok[b] = s_pe.bump()

                        nonlocal_last = [0]
                        for b in range(4 if not early else 0):
                            e01, ip = b // 2, b % 2
                            wsl = [(s_pe, up_bank_tok[b])]
                            if b == 0 and last_mult_tok:
                                wsl.append((s_v, last_mult_tok))
                            if is_real:
                                pd.op("scalar", lambda sc, e01=e01, ip=ip, b=b, pu=pu:
                                      sc.activation(
                                          silu_sb[:, (pu * 8 + e01 * 4 + 2 * ip) * NT2:
                                                  (pu * 8 + e01 * 4 + 2 * ip + 2) * NT2],
                                          psU[b][:, :], AF.Silu),
                                      waits=wsl, incs=[(s_a, 1)])
                            else:
                                pd.op("scalar", lambda sc, e01=e01, ip=ip, b=b, pu=pu:
                                      sc.activation(
                                          scaled_sb[:, (pu * 8 + e01 * 4 + 2 * ip) * NT2:
                                                    (pu * 8 + e01 * 4 + 2 * ip + 2) * NT2],
                                          psU[b][:, :], AF.Silu),
                                      waits=wsl + ([(s_pe, down_pe_tok[pu])]
                                                   if (b == 0 and down_pe_tok[pu]) else []),
                                      incs=[(s_a, 1)])
                            silu_tok[b] = (s_a, s_a.bump())

                        if is_real:
                            for e01 in range(2):
                                for ip in range(2):
                                    b = 2 * e01 + ip
                                    last = (e01 == 1 and ip == 1)
                                    inc = [(s_v, 1)] if last else []
                                    wml = [silu_tok[b]]
                                    if b == 0 and down_pe_tok[pu]:
                                        wml.append((s_pe, down_pe_tok[pu]))
                                    pd.op("vector", lambda v, e01=e01, ip=ip, pu=pu: v.tensor_tensor(
                                        scaled_sb[:, (pu * 8 + e01 * 4 + 2 * ip) * NT2:
                                                  (pu * 8 + e01 * 4 + 2 * ip + 2) * NT2].rearrange(
                                                      "p (i t) -> p i t", i=2),
                                        silu_sb[:, (pu * 8 + e01 * 4 + 2 * ip) * NT2:
                                                (pu * 8 + e01 * 4 + 2 * ip + 2) * NT2]
                                        .rearrange("p (i t) -> p i t", i=2),
                                        cbc_bf[:, (pu * 2 + e01) * NT2:(pu * 2 + e01 + 1) * NT2]
                                        .rearrange("p (o t) -> p o t", o=1)
                                        .to_broadcast([128, 2, NT2]),
                                        op=ALU.mult), waits=wml, incs=inc,
                                        drain=(e01 == 0 and ip == 0))
                            scaled_tok[pu] = s_v.bump()
                        else:
                            scaled_tok[pu] = 0

                        tile_silu[(p, u)] = (silu_tok[1], silu_tok[3])
                        return silu_tok[3]

                    def emit_back_tail(p, u, parity, is_real):
                        pu = u % 2
                        first_w = [(s_v, moe_evac_tok)] if moe_evac_tok else []
                        first_w.append(tile_silu[(p, u)][1])
                        bank_tok = [0] * 4
                        for hp in range(4):
                            for e01 in range(2):
                                for hh in range(2):
                                    h = 2 * hp + hh
                                    for i in range(4):
                                        first_all = (hp == 0 and e01 == 0
                                                     and hh == 0 and i == 0)
                                        last_b = (e01 == 1 and hh == 1 and i == 3)
                                        inc = [(s_pe, 1)] if last_b else []
                                        pd.op("tensor",
                                              lambda te, e01=e01, h=h, i=i, hp=hp,
                                              parity=parity, u=u, pu=pu,
                                              e0=(e01 == 0 and hh == 0 and i == 0),
                                              st=last_b: te.matmul(
                                                  psM[hp][:, (h % 2) * NT2:
                                                          (h % 2 + 1) * NT2],
                                                  wdn[parity][:, (e01 * 4 + i) * D
                                                              + h * 128:
                                                              (e01 * 4 + i) * D
                                                              + (h + 1) * 128],
                                                  scaled_sb[:, (pu * 8 + e01 * 4 + i)
                                                            * NT2:
                                                            (pu * 8 + e01 * 4 + i + 1)
                                                            * NT2],
                                                  start=e0, stop=st),
                                              waits=first_w if first_all else [],
                                              incs=inc)
                            bank_tok[hp] = s_pe.bump()
                            for hh in range(2):
                                h = 2 * hp + hh
                                pd.op("vector", lambda v, h=h, u=u, hp=hp: v.tensor_add(
                                    x1[:, US2(h, u)], x1[:, US2(h, u)],
                                    psM[hp][:, (h % 2) * NT2:(h % 2 + 1) * NT2]),
                                    waits=[(s_pe, bank_tok[hp])] if hh == 0 else [],
                                    incs=[(s_v, 1)], drain=(hh == 0))
                                htok = s_v.bump()
                                pd.op("sync", lambda sy, u=u, h=h: sy.dma_start(
                                    out_r[:, h, u * NT2:(u + 1) * NT2],
                                    x1v[:, h, u * NT2:(u + 1) * NT2]),
                                    waits=[(s_v, htok)], incs=[(s_o, 16)])
                                s_o.bump(16)
                        return bank_tok[3], s_v.n

                    def emit_back(p, u, parity, is_real):
                        nonlocal_moe = [0]
                        pu = u % 2
                        for e01 in range(2):
                            for h in range(DC):
                                bm = psM[h // 2]
                                off = (h % 2) * NT2
                                for i in range(4):
                                    first_all = (e01 == 0 and h == 0 and i == 0)
                                    last_all = (e01 == 1 and h == DC - 1 and i == 3)
                                    wdn_w = []
                                    if first_all:
                                        wdn_w = [(s_v, moe_evac_tok)] if moe_evac_tok else []
                                        if is_real:
                                            wdn_w.append((s_v, scaled_tok[pu]))
                                        else:
                                            wdn_w.append(tile_silu[(p, u)][1])
                                    inc = [(s_pe, 1)] if last_all else []
                                    pd.op("tensor",
                                          lambda te, e01=e01, h=h, i=i, bm=bm, off=off,
                                          parity=parity, u=u, pu=pu: te.matmul(
                                              bm[:, off:off + NT2],
                                              wdn[parity][:, (e01 * 4 + i) * D + h * 128:
                                                          (e01 * 4 + i) * D + (h + 1) * 128],
                                              scaled_sb[:, (pu * 8 + e01 * 4 + i) * NT2:
                                                        (pu * 8 + e01 * 4 + i + 1) * NT2],
                                              start=(e01 == 0 and i == 0 and h % 2 == 0),
                                              stop=(e01 == 1 and i == 3 and h % 2 == 1)),
                                          waits=wdn_w if first_all else [], incs=inc)
                        dtok = s_pe.bump()
                        down_pe_tok[pu] = dtok

                        last_pair = (p == NPAIR - 1)
                        for h in range(DC):
                            inc = [(s_v, 1)] if (last_pair or h == DC - 1) else []
                            wev = [(s_pe, dtok)] if h == 0 else []
                            if h == 0 and p == 0 and u == 0 and DEBUG_TAPS:
                                wev.append((s_dbg2, 16))
                            pd.op("vector", lambda v, h=h, u=u: v.tensor_add(
                                x1[:, US2(h, u)], x1[:, US2(h, u)],
                                psM[h // 2][:, (h % 2) * NT2:(h % 2 + 1) * NT2]),
                                waits=wev, incs=inc, drain=(h == 0))
                            if last_pair:
                                htok = s_v.bump()
                                pd.op("sync", lambda sy, u=u, h=h: sy.dma_start(
                                    out_r[:, h, u * NT2:(u + 1) * NT2],
                                    x1v[:, h, u * NT2:(u + 1) * NT2]),
                                    waits=[(s_v, htok)], incs=[(s_o, 16)])
                                s_o.bump(16)
                        mtok = s_v.n if last_pair else s_v.bump()
                        return dtok, mtok

                    for u in range(NTILE2):
                        emit_front(p, u, parity, is_real)
                        last_mult_tok = scaled_tok[u % 2] or last_mult_tok
                        prev = (p, u - 1, parity, is_real) if u >= 1 else prev_fb
                        if prev is not None:
                            down_tok, moe_evac_tok = emit_back(*prev)
                            if prev[1] == NTILE2 - 1 and prev[0] + 2 < NPAIR:
                                emit_pair_dma(pd, prev[0] + 2,
                                              waits=[(s_pe, down_tok)])
                    prev_fb = (p, NTILE2 - 1, parity, is_real)
                if prev_fb is not None:
                    p, u, parity, is_real = prev_fb
                    down_tok, moe_evac_tok = emit_back_tail(p, u, parity, is_real)

                wend = [(s_o, s_o.n)]
                if DEBUG_TAPS:
                    wend.append((s_dbg, s_dbg.n))
                    wend.append((s_dbg2, s_dbg2.n))
                pd.op("sync", lambda sy: sy.nop(), waits=wend)

                with nc.Block("ffn_pairs") as blk:
                    pd.replay(blk)

    return nc


def kernel(L0_attn_x, L0_attn_w_qkva, L0_attn_gamma, L0_ffn_gate_weight,
           L0_ffn_expert_up, L0_ffn_expert_down, L0_ffn_shared_up, L0_ffn_shared_down):
    x = np.asarray(L0_attn_x, dtype=np.float32)
    B = x.shape[0]
    wqT = np.ascontiguousarray(np.asarray(L0_attn_w_qkva, np.float32).T)
    gamma_pc = np.ascontiguousarray(np.asarray(L0_attn_gamma, np.float32).reshape(DC, 128).T)
    gateT = np.ascontiguousarray(np.asarray(L0_ffn_gate_weight, np.float32).T)
    eup = np.ascontiguousarray(np.asarray(L0_ffn_expert_up, np.float32))
    edn = np.ascontiguousarray(np.asarray(L0_ffn_expert_down, np.float32))
    sup = np.ascontiguousarray(np.asarray(L0_ffn_shared_up, np.float32))
    sdn = np.ascontiguousarray(np.asarray(L0_ffn_shared_down, np.float32))
    identc = np.eye(128, dtype=np.float32)
    onehotc = np.zeros((16, 16 * 128), dtype=np.float32)
    for e in range(16):
        onehotc[e, e * 128:(e + 1) * 128] = 1

    nc = build_program()
    in_maps = []
    for b in range(B):
        in_maps.append(dict(
            xt=np.ascontiguousarray(x[b].T), wqT=wqT, gamma_pc=gamma_pc, gateT=gateT,
            eup=eup, edn=edn, sup=sup, sdn=sdn, identc=identc, onehotc=onehotc))
    res = run_bass_kernel_spmd(nc, in_maps, core_ids=list(range(B)))
    outs = [np.ascontiguousarray(r["out"].T) for r in res.results]
    return np.stack(outs, axis=0)


if __name__ == "__main__":
    import reference as _r
    ins = _r.setup_inputs()
    got = kernel(**{k: np.asarray(v) for k, v in ins.items()})
    ref = np.asarray(_r.reference(**ins))
    err = np.linalg.norm(got - ref) / np.linalg.norm(ref)
    print("rel_err:", err)
